# version 15
# baseline (speedup 1.0000x reference)
import math
import numpy as np
import ml_dtypes
import concourse.bass as bass
import concourse.mybir as mybir
from concourse.bass import IndirectOffsetOnAxis
from concourse.bass_utils import run_bass_kernel_spmd

F32 = mybir.dt.float32
BF16 = mybir.dt.bfloat16
I32 = mybir.dt.int32
AF = mybir.ActivationFunctionType
ALU = mybir.AluOpType
AX = mybir.AxisListType.X

S = 2048
D = 1024
H = 16
DHR = 32
DLAT = 512
E = 8
DI = 4096
CAP = 1152
NSLOT = E * CAP
NT = S // 128
SCALE = 1.0 / math.sqrt(96.0)
CLO = math.exp(-80.0 * SCALE)
CHI = math.exp(80.0 * SCALE)
BLKS = [256, 256, 256, 256, 128]

ENGINES = ("sync", "gpsimd", "tensor", "vector", "scalar")


def rope_slot(h):
    return h // 4, 32 * (h % 4)


class SemC:
    __slots__ = ("h", "n", "name", "is_dma")

    def __init__(self, h, name):
        self.h = h
        self.n = 0
        self.name = name
        self.is_dma = False


class Prog:
    def __init__(self, nc):
        self.nc = nc
        self.streams = {e: [] for e in ENGINES}
        self.waited = {}
        self.eng_sems = {}
        self.all_sems = []

    def sem(self, name):
        s = SemC(self.nc.alloc_semaphore(name), name)
        self.all_sems.append(s)
        return s

    def emit(self, eng, build, waits=(), inc=None, k=1, raw=False):
        if raw and eng in self.eng_sems:
            s = self.eng_sems[eng]
            waits = list(waits) + [(s, s.n)]
        wl = []
        for w in waits:
            if w is None:
                continue
            s, v = (w if isinstance(w, tuple) else (w, w.n))
            if v <= 0:
                continue
            key = (eng, id(s))
            if self.waited.get(key, -1) < v:
                self.waited[key] = v
                wl.append((s.h, v))
        if inc is not None:
            inc.n += k
        hk = inc.h if inc is not None else None

        def th(e, build=build, wl=wl, hk=hk, k=k):
            for (h, v) in wl:
                e.wait_ge(h, v)
            ins = build(e)
            if hk is not None and ins is not None:
                ins.then_inc(hk, k)
        self.streams[eng].append(th)


def _cosf_sinf():
    inv = 10000.0 ** (-np.arange(0, DHR, 2, dtype=np.float64) / DHR)
    pos = np.arange(S, dtype=np.float64)
    ang = pos[None, :] * inv[:, None]
    cf = np.tile(np.cos(ang), (8, 1)).astype(ml_dtypes.bfloat16)
    sf = np.tile(np.sin(ang), (8, 1)).astype(ml_dtypes.bfloat16)
    return cf, sf


def _mtab():
    kp = np.arange(128)[:, None]
    c = np.arange(896)[None, :]
    return (kp <= c - 384).astype(ml_dtypes.bfloat16)


def build_core(upto=99, dbg=None):
    nc = bass.Bass("TRN2", target_bir_lowering=False)
    p = Prog(nc)

    dx = nc.dram_tensor("x", [S, D], F32, kind="ExternalInput").ap()
    dWDQ = nc.dram_tensor("wdq_t", [D, DLAT], F32, kind="ExternalInput").ap()
    dWDKV = nc.dram_tensor("wdkv_t", [D, DLAT], F32, kind="ExternalInput").ap()
    dWUQ = nc.dram_tensor("wuq_t", [DLAT, D], F32, kind="ExternalInput").ap()
    dWUK = nc.dram_tensor("wuk_t", [DLAT, D], F32, kind="ExternalInput").ap()
    dWUV = nc.dram_tensor("wuv_t", [DLAT, D], F32, kind="ExternalInput").ap()
    dWQ1 = nc.dram_tensor("wqr1", [DLAT, 512], F32, kind="ExternalInput").ap()
    dWQ2 = nc.dram_tensor("wqr2", [DLAT, 512], F32, kind="ExternalInput").ap()
    dWQ1s = nc.dram_tensor("wqr1s", [DLAT, 512], F32, kind="ExternalInput").ap()
    dWQ2s = nc.dram_tensor("wqr2s", [DLAT, 512], F32, kind="ExternalInput").ap()
    dWKR = nc.dram_tensor("wkrp", [D, 64], F32, kind="ExternalInput").ap()
    dWKRs = nc.dram_tensor("wkrps", [D, 64], F32, kind="ExternalInput").ap()
    dWO = nc.dram_tensor("wo_t", [D, D], F32, kind="ExternalInput").ap()
    dwr = nc.dram_tensor("wr", [D, E], F32, kind="ExternalInput").ap()
    dbr = nc.dram_tensor("br", [1, E], F32, kind="ExternalInput").ap()
    dw1 = nc.dram_tensor("w1", [E, D, DI], F32, kind="ExternalInput").ap()
    db1 = nc.dram_tensor("b1t", [E, 128, 32], F32, kind="ExternalInput").ap()
    dw2 = nc.dram_tensor("w2", [E, DI, D], F32, kind="ExternalInput").ap()
    db2 = nc.dram_tensor("b2", [E, D], F32, kind="ExternalInput").ap()
    dout = nc.dram_tensor("out", [S, D], F32, kind="ExternalOutput").ap()

    d_xn = nc.dram_tensor("xn_tok", [S, D], BF16, kind="Internal").ap()
    d_xg = nc.dram_tensor("xg", [NSLOT, D], BF16, kind="Internal").ap()
    d_ffn = nc.dram_tensor("ffn", [NSLOT, D], F32, kind="Internal").ap()
    d_base = nc.dram_tensor("base", [S, D], F32, kind="Internal").ap()

    cf_np, sf_np = _cosf_sinf()
    d_cf = nc.inline_tensor(np.ascontiguousarray(cf_np), "c_cf").ap()
    d_sf = nc.inline_tensor(np.ascontiguousarray(sf_np), "c_sf").ap()
    d_m = nc.inline_tensor(np.ascontiguousarray(_mtab()), "c_mtab").ap()
    d_id = nc.inline_tensor(np.eye(128, dtype=ml_dtypes.bfloat16), "c_ident").ap()
    ut = (np.arange(128)[:, None] < np.arange(128)[None, :]).astype(ml_dtypes.bfloat16)
    d_ut = nc.inline_tensor(ut, "c_utri").ap()
    d_on = nc.inline_tensor(np.ones((128, 128), dtype=ml_dtypes.bfloat16), "c_ones").ap()
    d_or = nc.inline_tensor(np.ones((1, 128), dtype=ml_dtypes.bfloat16), "c_onesrow").ap()
    d_orf = nc.inline_tensor(np.ones((1, 64), dtype=np.float32), "c_onesrowf").ap()
    iota_np = np.broadcast_to(np.arange(E, dtype=np.float32), (128, E)).copy()
    d_io = nc.inline_tensor(iota_np, "c_iota8").ap()

    TOP = nc.sbuf_top
    KB = 1024
    MAPKB = 207.0
    BASE0 = TOP - int(MAPKB * KB)
    assert BASE0 >= nc.sbuf_base, (BASE0, nc.sbuf_base)
    _placed = []

    def at(name, shape, dtype, off_kb):
        esz = {F32: 4, BF16: 2, I32: 4}[dtype]
        nbytes = int(np.prod(shape[1:])) * esz
        off = BASE0 + int(off_kb * KB)
        assert off + nbytes <= TOP, (name, off_kb, nbytes)
        _placed.append((name, off, off + nbytes))
        return nc.alloc_sbuf_tensor_at(name, list(shape), dtype, offset=off)

    f32a = at("f32a", [128, D], F32, 0)
    f32b = at("f32b", [128, D], F32, 4)
    ident = at("ident", [128, 128], BF16, 8)
    utri = at("utri", [128, 128], BF16, 8.25)
    ones = at("ones", [128, 128], BF16, 8.5)
    orow = at("orow", [1, 128], BF16, 8.75)
    orowf = at("orowf", [1, 64], F32, 9.0)
    iot = at("iota8", [128, E], F32, 9.25)
    logit = at("logit", [128, E], F32, 9.5)
    s8a = at("s8a", [128, E], F32, 9.5625)
    s8b = at("s8b", [128, E], F32, 9.625)
    m1t = at("m1t", [128, 16], F32, 9.6875)
    oh_f = at("oh_f", [128, NT * E], F32, 9.75)
    oh_b = at("oh_b", [128, NT * E], BF16, 10.25)
    gate = at("gate", [128, NT], F32, 10.5)
    idxf = at("idxf", [128, NT], F32, 10.5625)
    slot = at("slot", [128, NT], I32, 10.625)
    rankf = at("rankf", [128, E], F32, 10.6875)
    sltf = at("sltf", [128, 4], F32, 10.75)
    wrsb = at("wrsb", [128, E * 8], BF16, 10.8125)
    brsb = at("brsb", [1, E], BF16, 10.9375)
    cntf = at("cntf", [1, E], F32, 11.0)
    flgf = at("flgf", [1, 64], F32, 11.0625)
    flgi = at("flgi", [1, 64], I32, 11.3125)
    junk = at("junk", [1, 4], F32, 11.5625)
    kr12 = at("kr12", [32, S], BF16, 11.625)
    EB = 16.0
    xn_fm = at("xn_fm", [128, 8 * S], BF16, EB)
    ropeq = [at(f"ropeq{i}", [128, S], BF16, EB + 4 * i) for i in range(4)]
    ropet = [at(f"ropet{i}", [128, S], BF16, EB + 16 + 4 * i) for i in range(4)]
    attn_fm = at("attn_fm", [128, 8 * S], BF16, EB)
    c_q = at("c_q", [128, 4 * S], BF16, EB + 32)
    c_kv = at("c_kv", [128, 4 * S], BF16, EB + 48)
    q96 = at("q96", [128, 8 * S], BF16, EB + 64)
    wbufP1 = at("wbufP1", [128, 4096], BF16, EB + 64)
    kpre = at("kpre", [64, S], BF16, EB + 72)
    kpres = at("kpres", [64, S], BF16, EB + 76)
    ktmp = at("ktmp", [32, S], BF16, EB + 80)
    ktmp2 = at("ktmp2", [32, S], BF16, EB + 84)
    xnb0 = at("xnb0", [128, D], BF16, EB + 88)
    xnb1 = at("xnb1", [128, D], BF16, EB + 90)
    k96 = at("k96", [128, 8 * S], BF16, EB + 96)
    wbufP2 = at("wbufP2", [128, 4096], BF16, EB + 96)
    wbufP3 = at("wbufP3", [128, 4096], BF16, EB + 104)
    wobuf = at("wobuf", [128, 8 * D], BF16, EB + 96)
    cf = at("cf", [128, S], BF16, EB + 128)
    sf = at("sf", [128, S], BF16, EB + 132)
    v_a = at("v_a", [128, 8 * 1040], BF16, EB + 128)
    v_b = at("v_b", [128, 8 * 1040], BF16, EB + 144.25)
    rnka = at("rnka", [128, NT * E], F32, EB + 144.25)
    rsel = at("rsel", [128, NT], F32, EB + 144.75)
    ovr16 = at("ovr16", [128, NT], F32, EB + 144.8125)
    slf16 = at("slf16", [128, NT], F32, EB + 144.875)
    qr12 = at("qr12", [128, 4 * S], BF16, EB + 160.5)
    mtab = at("mtab", [128, 896], BF16, EB + 176.5)
    rcp = at("rcp", [1, 512], F32, EB + 178.5)
    rbsb = at("rbsb", [64, 512], F32, EB + 180.5)
    exps_a = at("exps_a", [128, 4 * 512], BF16, EB + 182.5)
    exps_b = at("exps_b", [128, 4 * 512], BF16, EB + 186.5)
    wbufQ = at("wbufQ", [128, 4096], BF16, EB + 182.5)
    wAa = at("wAa", [128, 16384], BF16, EB + 64)
    wAb = at("wAb", [128, 16384], BF16, EB + 96)
    wBa = at("wBa", [128, 16384], BF16, EB)
    wBb = at("wBb", [128, 16384], BF16, EB + 32)
    hmid = at("hmid", [128, 32 * 256], BF16, EB + 128)
    xg_st = at("xg_st", [128, 2 * D], BF16, EB + 144)
    xg_fm = at("xg_fm", [128, 8 * 256], BF16, EB + 148)
    b1sb = at("b1sb", [128, 64], F32, EB + 152)
    b2sb = at("b2sb", [1, 2 * D], BF16, EB + 152.25)
    f32c = at("f32c", [128, D], F32, EB + 176.5)
    f32d = at("f32d", [128, D], F32, EB + 180.5)

    pA = nc.alloc_psum_tensor("pA", [128, 512], F32)
    pB = nc.alloc_psum_tensor("pB", [128, 512], F32)
    pC = nc.alloc_psum_tensor("pC", [128, 512], F32)
    pD = nc.alloc_psum_tensor("pD", [128, 512], F32)
    pE = nc.alloc_psum_tensor("pE", [128, 512], F32)
    pF = nc.alloc_psum_tensor("pF", [128, 512], F32)
    pG = nc.alloc_psum_tensor("pG", [128, 512], F32)
    pT = nc.alloc_psum_tensor("pT", [128, 1024], BF16)

    s_const = p.sem("s_const")
    s_wr = p.sem("s_wr")
    s_x = [p.sem("s_x0"), p.sem("s_x1")]
    s_xnb = [p.sem("s_xnb0"), p.sem("s_xnb1")]
    s_sxnb = [p.sem("s_sxnb0"), p.sem("s_sxnb1")]
    s_scd = [p.sem("s_scd0"), p.sem("s_scd1")]
    s_sab = [p.sem("s_sab0"), p.sem("s_sab1")]
    s_w1 = p.sem("s_w1")
    s_w2 = p.sem("s_w2")
    s_w3 = p.sem("s_w3")
    s_wo = p.sem("s_wo")
    s_wAa = p.sem("s_wAa")
    s_wAb = p.sem("s_wAb")
    s_wBa = p.sem("s_wBa")
    s_wBb = p.sem("s_wBb")
    s_b1 = p.sem("s_b1")
    s_b2 = p.sem("s_b2")
    s_scat = [p.sem("s_scat0"), p.sem("s_scat1")]
    s_gath = [p.sem("s_gath0"), p.sem("s_gath1")]
    s_zero = p.sem("s_zero")
    s_wq = p.sem("s_wq")
    s_pk = p.sem("s_pk")
    s_xg = p.sem("s_xg")
    wbuf_sem = {}
    pe = p.sem("pe")
    act = p.sem("act")
    dve = p.sem("dve")
    p.eng_sems = {"vector": dve, "scalar": act}

    def sdma(eng, out_ap, in_ap, sem, waits=()):
        p.emit(eng, lambda e, o=out_ap, i=in_ap: e.dma_start(out=o, in_=i),
               waits=waits, inc=sem, k=16)

    def mark(sem):
        return (sem, sem.n)

    wbuf_sem.update({id(wbufP1): s_w1, id(wbufP2): s_w2, id(wbufP3): s_w3, id(wobuf): s_wo})

    bank = {id(b): {"free": None} for b in (pA, pB, pC, pD, pE, pF, pG, pT)}

    def load_w(buf, dram_ap, ncols, nchunk, extra=(), conservative=True):
        sem = wbuf_sem[id(buf)]
        w = ([mark(pe), mark(dve)] if conservative else []) + list(extra)
        sdma("gpsimd", buf[:, 0:ncols].rearrange("p (c o) -> p c o", c=nchunk), dram_ap,
             sem, waits=w)
        return mark(sem)

    sdma("sync", cf[:, :], d_cf[:, :], s_const)
    sdma("sync", sf[:, :], d_sf[:, :], s_const)
    sdma("sync", ident[:, :], d_id[:, :], s_const)
    sdma("sync", utri[:, :], d_ut[:, :], s_const)
    sdma("sync", ones[:, :], d_on[:, :], s_const)
    sdma("sync", orow[:, :], d_or[:, :], s_const)
    sdma("sync", orowf[:, :], d_orf[:, :], s_const)
    sdma("sync", iot[:, :], d_io[:, :], s_const)
    sdma("gpsimd", wrsb[:, :].rearrange("p (c e) -> p c e", c=8), dwr.rearrange("(c p) e -> p c e", p=128), s_wr)
    sdma("gpsimd", brsb[:, :], dbr[:, :], s_wr)
    consts_ld = mark(s_const)
    wr_ld = mark(s_wr)
    pool = p.sem("pool")
    p.emit("gpsimd", lambda e: e.memset(xnb0[:, :], 0.0), inc=pool)
    zmemset = mark(pool)
    for zi in range(NSLOT // 128):
        sdma("gpsimd", d_xg[zi * 128:(zi + 1) * 128, :], xnb0[:, :], s_zero, waits=[zmemset])
    zero_done = mark(s_zero)

    xt = [f32a, f32b]
    xnb = [xnb0, xnb1]
    xt_reader = [None, None]
    xnb_reader = [[zero_done], None]
    xn_store = []
    oh_done = []
    pa_free = None

    if upto >= 1:
        for t in range(NT):
            b = t % 2
            sdma("sync", xt[b][:, :], dx[t * 128:(t + 1) * 128, :], s_x[b],
                 waits=[xt_reader[b]])
            ldm = mark(s_x[b])
            Q = (t % 2) * 8
            p.emit("vector", lambda e, b=b, Q=Q: e.reduce_sum(m1t[:, Q:Q + 1], xt[b][:, :], axis=AX),
                   waits=[ldm], inc=dve, raw=True)
            p.emit("vector", lambda e, Q=Q: e.tensor_scalar_mul(m1t[:, Q + 1:Q + 2], m1t[:, Q:Q + 1],
                                                                1.0 / D), inc=dve, raw=True)
            p.emit("vector", lambda e, b=b, Q=Q: e.tensor_scalar(xt[b][:, :], xt[b][:, :],
                                                                 m1t[:, Q + 1:Q + 2], None,
                                                                 ALU.subtract), inc=dve, raw=True)
            cent = mark(dve)
            p.emit("scalar", lambda e, b=b, Q=Q: e.activation(xnb[b][:, :], xt[b][:, :], AF.Square,
                                                              accum_out=m1t[:, Q + 2:Q + 3]),
                   waits=[cent] + (xnb_reader[b] or []), inc=act, raw=True)
            sq = mark(act)
            p.emit("vector", lambda e, Q=Q: e.tensor_scalar(m1t[:, Q + 3:Q + 4], m1t[:, Q + 2:Q + 3],
                                                            1.0 / D, 1e-5,
                                                            ALU.mult, ALU.add), waits=[sq], inc=dve, raw=True)
            vpe = mark(dve)
            p.emit("scalar", lambda e, Q=Q: e.activation(m1t[:, Q + 1:Q + 2], m1t[:, Q + 3:Q + 4], AF.Ln),
                   waits=[vpe], inc=act, raw=True)
            p.emit("scalar", lambda e, Q=Q: e.activation(m1t[:, Q + 2:Q + 3], m1t[:, Q + 1:Q + 2],
                                                         AF.Exp, scale=-0.5), inc=act, raw=True)
            rstd = mark(act)
            p.emit("vector", lambda e, b=b, Q=Q: e.tensor_scalar(xnb[b][:, :], xt[b][:, :],
                                                                 m1t[:, Q + 2:Q + 3], None,
                                                                 ALU.mult), waits=[rstd], inc=dve, raw=True)
            xnm = mark(dve)
            xt_reader[b] = xnm
            sdma("sync", d_xn[t * 128:(t + 1) * 128, :], xnb[b][:, :], s_sxnb[b], waits=[xnm])
            xn_store.append(mark(s_sxnb[b]))
            for c in range(8):
                tb = pT[:, 0:128] if c % 2 == 0 else pC[:, 0:64].bitcast(BF16)
                tkey = id(pT) if c % 2 == 0 else id(pC)
                p.emit("tensor", lambda e, b=b, c=c, tb=tb: e.transpose(tb, xnb[b][:, c * 128:(c + 1) * 128],
                                                                        ident[:, :]),
                       waits=[xnm, consts_ld, bank[tkey]["free"]], inc=pe)
                tm = mark(pe)
                if c % 2 == 0:
                    p.emit("scalar", lambda e, c=c, t=t, tb=tb: e.copy(
                        xn_fm[:, c * S + t * 128: c * S + (t + 1) * 128], tb),
                        waits=[tm], inc=act, raw=True)
                    bank[tkey]["free"] = mark(act)
                else:
                    p.emit("vector", lambda e, c=c, t=t, tb=tb: e.tensor_copy(
                        xn_fm[:, c * S + t * 128: c * S + (t + 1) * 128], tb),
                        waits=[tm], inc=dve, raw=True)
                    bank[tkey]["free"] = mark(dve)
            xnb_reader[b] = [mark(act), mark(dve), mark(s_sxnb[b])]
            xnfm_t = [mark(act), mark(dve)]
            for c in range(8):
                p.emit("tensor", lambda e, c=c, t=t: e.matmul(pA[:, 0:E],
                                                              xn_fm[:, c * S + t * 128: c * S + (t + 1) * 128],
                                                              wrsb[:, c * E:(c + 1) * E],
                                                              start=(c == 0), stop=False),
                       waits=xnfm_t + [wr_ld, pa_free], inc=pe)
            p.emit("tensor", lambda e: e.matmul(pA[:, 0:E], orow[:, :], brsb[:, :], start=False, stop=True),
                   waits=[consts_ld], inc=pe)
            rmm = mark(pe)
            p.emit("vector", lambda e: e.tensor_copy(logit[:, :], pA[:, 0:E]), waits=[rmm], inc=dve, raw=True)
            pa_free = mark(dve)
            p.emit("vector", lambda e, Q=Q: e.reduce_max(m1t[:, Q + 4:Q + 5], logit[:, :], axis=AX), inc=dve, raw=True)
            p.emit("vector", lambda e, t=t, Q=Q: e.tensor_scalar(oh_f[:, t * E:(t + 1) * E], logit[:, :],
                                                                 m1t[:, Q + 4:Q + 5], 0.0,
                                                                 ALU.subtract, ALU.is_ge), inc=dve, raw=True)
            p.emit("vector", lambda e, t=t: e.tensor_copy(oh_b[:, t * E:(t + 1) * E], oh_f[:, t * E:(t + 1) * E]),
                   inc=dve, raw=True)
            p.emit("vector", lambda e, t=t: e.tensor_tensor(s8a[:, :], oh_f[:, t * E:(t + 1) * E], iot[:, :],
                                                            ALU.mult), waits=[consts_ld], inc=dve, raw=True)
            p.emit("vector", lambda e, t=t: e.reduce_sum(idxf[:, t:t + 1], s8a[:, :], axis=AX), inc=dve, raw=True)
            p.emit("vector", lambda e, t=t: e.scalar_tensor_tensor(s8b[:, :], oh_f[:, t * E:(t + 1) * E], -1e30,
                                                                   logit[:, :], ALU.mult, ALU.add), inc=dve, raw=True)
            p.emit("vector", lambda e, Q=Q: e.reduce_max(m1t[:, Q + 5:Q + 6], s8b[:, :], axis=AX), inc=dve, raw=True)
            p.emit("vector", lambda e, Q=Q: e.tensor_tensor(m1t[:, Q + 6:Q + 7], m1t[:, Q + 4:Q + 5],
                                                            m1t[:, Q + 5:Q + 6], ALU.subtract),
                   inc=dve, raw=True)
            dl = mark(dve)
            p.emit("scalar", lambda e, Q=Q: e.activation(m1t[:, Q + 7:Q + 8], m1t[:, Q + 6:Q + 7],
                                                         AF.Exp, scale=-1.0),
                   waits=[dl], inc=act, raw=True)
            ge = mark(act)
            p.emit("vector", lambda e, Q=Q: e.tensor_scalar(m1t[:, Q + 6:Q + 7], m1t[:, Q + 7:Q + 8],
                                                            1.0, None, ALU.add),
                   waits=[ge], inc=dve, raw=True)
            p.emit("vector", lambda e, t=t, Q=Q: e.reciprocal(gate[:, t:t + 1], m1t[:, Q + 6:Q + 7]), inc=dve, raw=True)
            oh_done.append(mark(dve))

    scat_done = None
    if upto >= 2:
        allo = oh_done[-1]
        pb_free = None
        for t in range(NT):
            p.emit("tensor", lambda e, t=t: e.matmul(pB[:, 0:E], utri[:, :], oh_b[:, t * E:(t + 1) * E],
                                                     start=True, stop=(t == 0)),
                   waits=[allo, consts_ld, pb_free], inc=pe)
            for tp in range(t):
                p.emit("tensor", lambda e, tp=tp, t=t: e.matmul(pB[:, 0:E], ones[:, :],
                                                                oh_b[:, tp * E:(tp + 1) * E],
                                                                start=False, stop=(tp == t - 1)), inc=pe)
            rk = mark(pe)
            p.emit("vector", lambda e, t=t: e.tensor_copy(rnka[:, t * E:(t + 1) * E], pB[:, 0:E]),
                   waits=[rk], inc=dve, raw=True)
            pb_free = mark(dve)
        p.emit("vector", lambda e: e.tensor_tensor(oh_f[:, :], rnka[:, :], oh_f[:, :], ALU.mult),
               inc=dve, raw=True)
        p.emit("vector", lambda e: e.reduce_sum(rsel[:, :],
                                                oh_f[:, :].rearrange("p (t e) -> p t e", e=E),
                                                axis=AX), inc=dve, raw=True)
        p.emit("vector", lambda e: e.scalar_tensor_tensor(slf16[:, :], idxf[:, :], float(CAP),
                                                          rsel[:, :], ALU.mult, ALU.add), inc=dve, raw=True)
        p.emit("vector", lambda e: e.tensor_scalar(ovr16[:, :], rsel[:, :], float(CAP) - 0.5, 0.0,
                                                   ALU.subtract, ALU.is_ge), inc=dve, raw=True)
        p.emit("vector", lambda e: e.scalar_tensor_tensor(slf16[:, :], ovr16[:, :], 1e7,
                                                          slf16[:, :], ALU.mult, ALU.add), inc=dve, raw=True)
        p.emit("vector", lambda e: e.tensor_copy(slot[:, :], slf16[:, :]), inc=dve, raw=True)
        sl = mark(dve)
        for t in range(NT):
            b = t % 2
            sdma("sync", xnb[b][:, :], d_xn[t * 128:(t + 1) * 128, :], s_xnb[b],
                 waits=[xn_store[t]] + (xnb_reader[b] or []))
            xr = mark(s_xnb[b])
            p.emit("gpsimd", lambda e, t=t, b=b: e.indirect_dma_start(
                d_xg[0:NSLOT, :], IndirectOffsetOnAxis(slot[:, t:t + 1], 0),
                xnb[b][:, :], None,
                bounds_check=NSLOT - 1, oob_is_err=False),
                waits=[sl, xr, zero_done], inc=s_scat[b], k=16)
            xnb_reader[b] = [mark(s_scat[b])]
        scat_done = [mark(s_scat[0]), mark(s_scat[1])]
        for t in range(NT):
            p.emit("tensor", lambda e, t=t: e.matmul(pC[0:1, 0:E], ones[:, 0:1],
                                                     oh_b[:, t * E:(t + 1) * E],
                                                     start=(t == 0), stop=(t == NT - 1)),
                   waits=[allo, consts_ld], inc=pe)
        cmm = mark(pe)
        p.emit("vector", lambda e: e.tensor_copy(cntf[:, :], pC[0:1, 0:E]), waits=[cmm], inc=dve, raw=True)
        bank[id(pC)]["free"] = mark(dve)
        boff_acc = 0
        for b5 in range(len(BLKS)):
            p.emit("vector", lambda e, b5=b5, bo=boff_acc: e.tensor_scalar(
                flgf[0:1, b5 * E:(b5 + 1) * E], cntf[0:1, :], bo + 0.5, 0.0,
                ALU.subtract, ALU.is_ge), inc=dve, raw=True)
            boff_acc += BLKS[b5]
        p.emit("vector", lambda e: e.tensor_copy(flgi[0:1, 0:len(BLKS) * E], flgf[0:1, 0:len(BLKS) * E]),
               inc=dve, raw=True)
        flag_gate = mark(dve)

    def proj_fm(dst, w_dram, wbuf, kdim, odim, rhs, rhs_ready, psums, dve_out=True, first=False):
        kc = kdim // 128
        oc = odim // 128
        wl = load_w(wbuf, w_dram.rearrange("(c p) o -> p c o", p=128), kc * odim, kc,
                    conservative=not first)
        done = None
        for o in range(oc):
            for tt in range(4):
                bk_t = psums[(o * 4 + tt) % len(psums)]
                bk = bank[id(bk_t)]
                for k in range(kc):
                    p.emit("tensor", lambda e, o=o, tt=tt, k=k, bk_t=bk_t, wbuf=wbuf, odim=odim, kc=kc, rhs=rhs:
                           e.matmul(bk_t[:, :],
                                    wbuf[:, k * odim + o * 128: k * odim + (o + 1) * 128],
                                    rhs[:, k * S + tt * 512: k * S + (tt + 1) * 512],
                                    start=(k == 0), stop=(k == kc - 1)),
                           waits=[wl, bk["free"]] + (rhs_ready if isinstance(rhs_ready, list) else [rhs_ready]), inc=pe)
                mm = mark(pe)
                p.emit("vector", lambda e, o=o, tt=tt, bk_t=bk_t, dst=dst: e.tensor_copy(
                    dst[:, o * S + tt * 512: o * S + (tt + 1) * 512], bk_t[:, :]),
                    waits=[mm], inc=dve, raw=True)
                bk["free"] = mark(dve)
        return mark(dve)

    if upto >= 3:
        xn_last = mark(act)
        xn_last_d = mark(dve)
        cq_done = proj_fm(c_q, dWDQ, wbufP1, D, DLAT, xn_fm, [xn_last, xn_last_d], (pA, pB),
                          first=True)
        ckv_done = proj_fm(c_kv, dWDKV, wbufP1, D, DLAT, xn_fm, [xn_last, xn_last_d], (pA, pB))
        for (wd, dst) in ((dWKR, kpre), (dWKRs, kpres)):
            wl = load_w(wbufP1, wd.rearrange("(c p) o -> p c o", p=128), 8 * 64, 8)
            for tt in range(4):
                bk = bank[id(pC)]
                for k in range(8):
                    p.emit("tensor", lambda e, tt=tt, k=k: e.matmul(
                        pC[0:64, :], wbufP1[:, k * 64:(k + 1) * 64],
                        xn_fm[:, k * S + tt * 512: k * S + (tt + 1) * 512],
                        start=(k == 0), stop=(k == 7)),
                        waits=[wl, xn_last, xn_last_d, bk["free"]], inc=pe)
                mm = mark(pe)
                p.emit("vector", lambda e, dst=dst, tt=tt: e.tensor_copy(dst[:, tt * 512:(tt + 1) * 512],
                                                                         pC[0:64, :]),
                       waits=[mm], inc=dve, raw=True)
                bk["free"] = mark(dve)
        kp_done = mark(dve)
        p.emit("vector", lambda e: e.tensor_tensor(ktmp[:, :], kpre[0:32, :], cf[0:32, :], ALU.mult),
               waits=[kp_done, consts_ld], inc=dve, raw=True)
        p.emit("vector", lambda e: e.tensor_tensor(ktmp2[:, :], kpre[32:64, :], sf[32:64, :], ALU.mult), inc=dve, raw=True)
        p.emit("vector", lambda e: e.tensor_tensor(ktmp[:, :], ktmp[:, :], ktmp2[:, :], ALU.subtract), inc=dve, raw=True)
        p.emit("vector", lambda e: e.tensor_tensor(ktmp2[:, :], kpres[32:64, :], sf[32:64, :], ALU.mult), inc=dve, raw=True)
        p.emit("vector", lambda e: e.tensor_tensor(kr12[0:32, :], kpres[0:32, :], cf[0:32, :], ALU.mult), inc=dve, raw=True)
        p.emit("vector", lambda e: e.tensor_tensor(ktmp2[:, :], ktmp2[:, :], kr12[0:32, :], ALU.add), inc=dve, raw=True)
        p.emit("vector", lambda e: e.tensor_tensor(kr12[0:32, :], ktmp[:, :], ktmp2[:, :], ALU.add), inc=dve, raw=True)
        kr_done = mark(dve)

    if upto >= 4:
        rope_cnt = [0]
        rope_buf_reader = {id(wbufP2): None, id(wbufP3): None}

        def rope_proj(dst_list, w_dram):
            wb = (wbufP2, wbufP3)[rope_cnt[0] % 2]
            rope_cnt[0] += 1
            wl = load_w(wb, w_dram.rearrange("(c p) o -> p c o", p=128), 4 * 512, 4,
                        conservative=False, extra=[rope_buf_reader[id(wb)]])
            for c in range(4):
                for tt in range(4):
                    bk_t = (pA, pB)[tt % 2]
                    bk = bank[id(bk_t)]
                    for k in range(4):
                        p.emit("tensor", lambda e, c=c, tt=tt, k=k, bk_t=bk_t, wb=wb: e.matmul(
                            bk_t[:, :], wb[:, k * 512 + c * 128: k * 512 + (c + 1) * 128],
                            c_q[:, k * S + tt * 512: k * S + (tt + 1) * 512],
                            start=(k == 0), stop=(k == 3)),
                            waits=[wl, cq_done, bk["free"]], inc=pe)
                    mm = mark(pe)
                    p.emit("vector", lambda e, dst=dst_list[c], tt=tt, bk_t=bk_t: e.tensor_copy(
                        dst[:, tt * 512:(tt + 1) * 512], bk_t[:, :]), waits=[mm], inc=dve, raw=True)
                    bk["free"] = mark(dve)
            rope_buf_reader[id(wb)] = mark(pe)
            return mark(dve)

        q1_done = rope_proj(ropeq, dWQ1)
        q2_done = rope_proj(ropet, dWQ2)
        for c in range(4):
            p.emit("vector", lambda e, c=c: e.tensor_tensor(qr12[:, c * S:(c + 1) * S], ropeq[c][:, :],
                                                            cf[:, :], ALU.mult),
                   waits=[q1_done, q2_done, consts_ld], inc=dve, raw=True)
            p.emit("vector", lambda e, c=c: e.tensor_tensor(ropeq[c][:, :], ropet[c][:, :], sf[:, :],
                                                            ALU.mult), inc=dve, raw=True)
            p.emit("vector", lambda e, c=c: e.tensor_tensor(qr12[:, c * S:(c + 1) * S],
                                                            qr12[:, c * S:(c + 1) * S], ropeq[c][:, :],
                                                            ALU.subtract), inc=dve, raw=True)
        part1 = mark(dve)
        q1s_done = rope_proj(ropeq, dWQ1s)
        q2s_done = rope_proj(ropet, dWQ2s)
        for c in range(4):
            p.emit("vector", lambda e, c=c: e.tensor_tensor(ropeq[c][:, :], ropeq[c][:, :], sf[:, :], ALU.mult),
                   waits=[q1s_done, q2s_done, part1], inc=dve, raw=True)
            p.emit("vector", lambda e, c=c: e.tensor_tensor(ropet[c][:, :], ropet[c][:, :], cf[:, :], ALU.mult),
                   inc=dve, raw=True)
            p.emit("vector", lambda e, c=c: e.tensor_tensor(ropeq[c][:, :], ropeq[c][:, :], ropet[c][:, :],
                                                            ALU.add), inc=dve, raw=True)
            p.emit("vector", lambda e, c=c: e.tensor_tensor(qr12[:, c * S:(c + 1) * S],
                                                            qr12[:, c * S:(c + 1) * S], ropeq[c][:, :],
                                                            ALU.add), inc=dve, raw=True)
        qr_done = mark(dve)

        wlv = load_w(wbufP3, dWUV.rearrange("(c p) o -> p c o", p=128), 4 * D, 4)
        p.emit("vector", lambda e: e.memset(v_a[:, :], 1.0), waits=[mark(dve), qr_done], inc=dve, raw=True)
        p.emit("vector", lambda e: e.memset(v_b[:, :], 1.0), inc=dve, raw=True)
        vset = mark(dve)
        for kt in range(NT):
            vten = v_a if kt < 8 else v_b
            ktc = kt % 8
            for ob in range(2):
                bk_t = (pC, pD)[ob]
                bk = bank[id(bk_t)]
                for k in range(4):
                    p.emit("tensor", lambda e, kt=kt, ob=ob, k=k, bk_t=bk_t: e.matmul(
                        bk_t[:, :], c_kv[:, k * S + kt * 128: k * S + (kt + 1) * 128],
                        wbufP3[:, k * D + ob * 512: k * D + (ob + 1) * 512],
                        start=(k == 0), stop=(k == 3)),
                        waits=[wlv, ckv_done, bk["free"]], inc=pe)
                mm = mark(pe)
                for hh in range(8):
                    h = ob * 8 + hh
                    p.emit("vector", lambda e, vten=vten, ktc=ktc, h=h, hh=hh, bk_t=bk_t: e.tensor_copy(
                        vten[:, ktc * 1040 + h * 65: ktc * 1040 + h * 65 + 64],
                        bk_t[:, hh * 64:(hh + 1) * 64]),
                        waits=[mm, vset], inc=dve, raw=True)
                bk["free"] = mark(dve)
        v_done = mark(dve)
        sdma("sync", mtab[:, :], d_m[:, :], s_const, waits=[mark(pe)])
        mtab_ld = mark(s_const)

    if upto >= 5:
        sc_banks = (pA, pB, pF, pG)
        LAG = 3
        tail_pend = [None]
        rcp_reader = [None]
        exps_reader = {}
        av_banks = (pC, pE)
        for ps in range(2):
            prev_attn = [mark(pe), mark(act), mark(dve)]
            proj_evac = []
            for (wd, dstT, srcT, src_ready) in ((dWUQ, q96, c_q, cq_done),
                                                (dWUK, k96, c_kv, ckv_done)):
                sdma("gpsimd", wbufQ[:, :].rearrange("p (c o) -> p c o", c=4),
                     wd.rearrange("(c p) o -> p c o", p=128), s_wq,
                     waits=prev_attn + [mark(pe)])
                wl = mark(s_wq)
                for i in range(8):
                    h = 8 * ps + i
                    for tt in range(4):
                        bk_t = (pA, pB)[(i * 4 + tt) % 2]
                        bk = bank[id(bk_t)]
                        for k in range(4):
                            p.emit("tensor", lambda e, h=h, tt=tt, k=k, bk_t=bk_t, srcT=srcT: e.matmul(
                                bk_t[0:64, :], wbufQ[:, k * D + 64 * h: k * D + 64 * h + 64],
                                srcT[:, k * S + tt * 512: k * S + (tt + 1) * 512],
                                start=(k == 0), stop=(k == 3)),
                                waits=[wl, src_ready, bk["free"]], inc=pe)
                        mm = mark(pe)
                        p.emit("vector", lambda e, i=i, tt=tt, bk_t=bk_t, dstT=dstT: e.tensor_copy(
                            dstT[0:64, i * S + tt * 512: i * S + (tt + 1) * 512], bk_t[0:64, :]),
                            waits=[mm, scat_done[0], scat_done[1]], inc=dve, raw=True)
                        bk["free"] = mark(dve)
            proj_done = mark(dve)
            for i in range(8):
                h = 8 * ps + i
                rc4, rb4 = rope_slot(h)
                sdma("sync", q96[64:96, i * S:(i + 1) * S],
                     qr12[rb4:rb4 + 32, rc4 * S:(rc4 + 1) * S], s_pk,
                     waits=[qr_done] + prev_attn)
                sdma("sync", k96[64:96, i * S:(i + 1) * S], kr12[0:32, :], s_pk,
                     waits=[kr_done] + prev_attn)
            pack_done = mark(s_pk)
            ready = [proj_done, pack_done, v_done, consts_ld, mtab_ld]
            for i in range(8):
                h = 8 * ps + i
                for qb in range(4):
                    nkt = 4 * qb + 4
                    avb = av_banks[(h * 4 + qb) % 2]
                    avbk = bank[id(avb)]
                    pend = []

                    def emit_av(h=h, qb=qb, nkt=nkt, avb=avb, avbk=avbk):
                        kt, sl, use_a, rdy = pend.pop(0)
                        src = exps_a if use_a else exps_b
                        p.emit("tensor", lambda e, h=h, kt=kt, sl=sl, src=src, avb=avb, nkt=nkt: e.matmul(
                            avb[0:65, :],
                            (v_a if kt < 8 else v_b)[:, (kt % 8) * 1040 + h * 65: (kt % 8) * 1040 + h * 65 + 65],
                            src[:, sl * 512:(sl + 1) * 512],
                            start=(kt == 0), stop=(kt == nkt - 1)),
                            waits=[rdy] + ([avbk["free"]] if kt == 0 else []), inc=pe)
                        exps_reader[sl] = mark(pe)

                    for kt in range(nkt):
                        sl = kt % 4
                        sb = sc_banks[kt % 4]
                        sbk = bank[id(sb)]
                        p.emit("tensor", lambda e, i=i, kt=kt, qb=qb, sb=sb: e.matmul(
                            sb[:, :], k96[0:96, i * S + kt * 128: i * S + (kt + 1) * 128],
                            q96[0:96, i * S + qb * 512: i * S + (qb + 1) * 512],
                            start=True, stop=True),
                            waits=ready + [sbk["free"]], inc=pe)
                        smm = mark(pe)
                        p.emit("scalar", lambda e, sl=sl, sb=sb: e.activation(
                            exps_a[:, sl * 512:(sl + 1) * 512], sb[:, :], AF.Exp, scale=SCALE),
                            waits=[smm, exps_reader.get(sl)], inc=act, raw=True)
                        exm = mark(act)
                        sbk["free"] = exm
                        p.emit("vector", lambda e, sl=sl: e.tensor_scalar(
                            exps_b[:, sl * 512:(sl + 1) * 512], exps_a[:, sl * 512:(sl + 1) * 512],
                            CLO, CHI, ALU.max, ALU.min),
                            waits=[exm], inc=dve)
                        partial = kt >= 4 * qb
                        if partial:
                            mo = qb * 512 - kt * 128 + 384
                            p.emit("vector", lambda e, sl=sl, mo=mo: e.tensor_tensor(
                                exps_a[:, sl * 512:(sl + 1) * 512], exps_b[:, sl * 512:(sl + 1) * 512],
                                mtab[:, mo:mo + 512], ALU.mult),
                                waits=[mtab_ld], inc=dve, raw=True)
                        pend.append((kt, sl, partial, mark(dve)))
                        if kt == 3 and tail_pend[0] is not None:
                            tail_pend[0]()
                            tail_pend[0] = None
                        if len(pend) > LAG:
                            emit_av()
                    if tail_pend[0] is not None:
                        tail_pend[0]()
                        tail_pend[0] = None
                    while pend:
                        emit_av()
                    avm = mark(pe)
                    p.emit("vector", lambda e, avb=avb: e.reciprocal(rcp[:, :], avb[64:65, :]),
                           waits=[avm, rcp_reader[0]], inc=dve, raw=True)
                    rc = mark(dve)

                    def emit_tail(h=h, qb=qb, avb=avb, avbk=avbk, rc=rc):
                        ch = h // 2
                        rb = 64 * (h % 2)
                        bkD = bank[id(pD)]
                        p.emit("tensor", lambda e: e.matmul(pD[0:64, :], orowf[:, :], rcp[:, :],
                                                            start=True, stop=True),
                               waits=[rc, bkD["free"], consts_ld], inc=pe)
                        bc = mark(pe)
                        rcp_reader[0] = bc
                        p.emit("vector", lambda e: e.tensor_copy(rbsb[:, :], pD[0:64, :]),
                               waits=[bc], inc=dve, raw=True)
                        bkD["free"] = mark(dve)
                        p.emit("vector", lambda e, ch=ch, rb=rb, qb=qb, avb=avb: e.tensor_tensor(
                            attn_fm[rb:rb + 64, ch * S + qb * 512: ch * S + (qb + 1) * 512],
                            avb[0:64, :], rbsb[:, :], ALU.mult), inc=dve, raw=True)
                        avbk["free"] = mark(dve)
                    tail_pend[0] = emit_tail
        if tail_pend[0] is not None:
            tail_pend[0]()
            tail_pend[0] = None
        attn_done = mark(dve)
        attn_pe_end = mark(pe)

    if upto >= 6:
        wlo = load_w(wobuf, dWO.rearrange("(c p) o -> p c o", p=128), 8 * D, 8,
                     extra=[mark(act)])
        wo_banks = (pA, pB, pF, pG)
        st_bufs = [f32c, f32d]
        st_reader = [None, None]
        base_stores = []
        for t in range(NT):
            b = t % 2
            sdma("sync", xt[b][:, :], dx[t * 128:(t + 1) * 128, :], s_x[b], waits=[xt_reader[b]])
            xl = mark(s_x[b])
            both = []
            for dh in range(2):
                bk_t = wo_banks[(t * 2 + dh) % 4]
                bk = bank[id(bk_t)]
                for c in range(8):
                    p.emit("tensor", lambda e, t=t, dh=dh, c=c, bk_t=bk_t: e.matmul(
                        bk_t[:, :], attn_fm[:, c * S + t * 128: c * S + (t + 1) * 128],
                        wobuf[:, c * D + dh * 512: c * D + (dh + 1) * 512],
                        start=(c == 0), stop=(c == 7)),
                        waits=[wlo, attn_done, bk["free"]], inc=pe)
                mm = mark(pe)
                p.emit("vector", lambda e, b=b, dh=dh, bk_t=bk_t: e.tensor_tensor(
                    st_bufs[b][:, dh * 512:(dh + 1) * 512], bk_t[:, :],
                    xt[b][:, dh * 512:(dh + 1) * 512], ALU.add),
                    waits=[mm, xl, st_reader[b]], inc=dve, raw=True)
                bk["free"] = mark(dve)
            bs = mark(dve)
            xt_reader[b] = bs
            sdma("sync", d_base[t * 128:(t + 1) * 128, :], st_bufs[b][:, :], s_scd[b], waits=[bs])
            st_reader[b] = mark(s_scd[b])
            base_stores.append(mark(s_scd[b]))
        base_done = [mark(s_scd[0]), mark(s_scd[1])]

    ETE = mybir.EngineType
    eng_et = {"sync": ETE.SP, "tensor": ETE.PE, "scalar": ETE.Activation,
              "vector": ETE.DVE, "gpsimd": ETE.Pool}
    guard_state = {}

    def if_open(eng, fidx, gate):
        guard_state[eng] = {k: v for k, v in p.waited.items() if k[0] == eng}
        cell = {}
        guard_state[(eng, "cell")] = cell

        def th(e, fidx=fidx, gate=gate, et=eng_et[eng], cell=cell):
            e.wait_ge(gate[0].h, gate[1])
            val = nc.values_load(flgi[0:1, fidx:fidx + 1], engines=[et],
                                 min_val=0, max_val=1, skip_runtime_bounds_check=True)
            g = nc.If(val)
            g.__enter__()
            cell["g"] = g
        p.streams[eng].append(th)

    def if_close(eng, deltas=None):
        cell = guard_state[(eng, "cell")]

        def th(e, cell=cell, deltas=deltas):
            cell["g"].__exit__(None, None, None)
            if deltas:
                ge = nc.Else()
                ge.__enter__()
                for (h, d) in deltas:
                    e.memset(junk[0:1, 0:1], 0.0).then_inc(h, d)
                ge.__exit__(None, None, None)
        p.streams[eng].append(th)
        snap = guard_state[eng]
        for k in [k for k in p.waited if k[0] == eng]:
            del p.waited[k]
        p.waited.update(snap)

    if upto >= 7:
        wA_m = wB_m = b1_m = b2_m = None
        cur_par = 0

        def load_w1(ei, waits, waits_b=None):
            nonlocal wA_m, b1_m, b2_m
            sdma("gpsimd", wAa[:, :].rearrange("p (c i) -> p c i", c=8),
                 dw1[ei, :, 0:2048].rearrange("(c p) i -> p c i", p=128), s_wAa, waits=waits)
            sdma("gpsimd", wAb[:, :].rearrange("p (c i) -> p c i", c=8),
                 dw1[ei, :, 2048:4096].rearrange("(c p) i -> p c i", p=128), s_wAb,
                 waits=(waits_b if waits_b is not None else waits))
            wA_m = [mark(s_wAa), mark(s_wAb)]
            par = ei % 2
            sdma("sync", b1sb[:, par * 32:(par + 1) * 32], db1[ei, :, :], s_b1, waits=waits)
            b1_m = mark(s_b1)
            sdma("gpsimd", b2sb[:, par * D:(par + 1) * D], db2[ei:ei + 1, :], s_b2, waits=waits)
            b2_m = mark(s_b2)

        def load_w2(ei, waits, waits_a=None):
            nonlocal wB_m
            sdma("gpsimd", wBa[:, :].rearrange("p (c o) -> p c o", c=16),
                 dw2[ei, 0:2048, :].rearrange("(c p) o -> p c o", p=128), s_wBa,
                 waits=(waits_a if waits_a is not None else waits))
            sdma("gpsimd", wBb[:, :].rearrange("p (c o) -> p c o", c=16),
                 dw2[ei, 2048:4096, :].rearrange("(c p) o -> p c o", p=128), s_wBb, waits=waits)
            wB_m = [mark(s_wBa), mark(s_wBb)]

        def load_expert(ei, waits):
            load_w1(ei, waits)
            load_w2(ei, waits)

        def w1_tile(k, ic):
            ten = wAa if ic < 16 else wAb
            return ten[:, k * 2048 + (ic % 16) * 128: k * 2048 + (ic % 16) * 128 + 128]

        def w2_tile(ic2, hh):
            ten = wBa if ic2 < 16 else wBb
            return ten[:, (ic2 % 16) * D + hh * 512: (ic2 % 16) * D + (hh + 1) * 512]

        load_w1(0, [attn_pe_end] + scat_done,
                waits_b=[mark(pe)] + scat_done)
        load_w2(0, [mark(pe), mark(act), mark(dve)] + base_done)
        xg_reader = attn_pe_end
        hm_banks = (pA, pB)
        ho_banks = (pC, pD, pE, pF)
        hout_buf = [f32c, f32d]
        hout_reader = [st_reader[0], st_reader[1]]
        ffn_stores = []
        for ei in range(E):
            par = ei % 2
            soff = ei * CAP
            boff = 0
            wA_cur, b1_cur, b2_cur = wA_m, b1_m, b2_m
            wB_cur = wB_m
            w1_gate = w1_gate_a = None
            w1a_mark = w2a_mark = None
            for bi, bs_n in enumerate(BLKS):
                nct = bs_n // 128
                fidx = bi * E + ei
                for ge_ in ("sync", "tensor", "scalar", "vector"):
                    if_open(ge_, fidx, flag_gate)
                sem_snap = {id(s): s.n for s in p.all_sems}
                for c in range(nct):
                    sdma("sync", xg_st[:, c * D:(c + 1) * D],
                         d_xg[soff + boff + c * 128: soff + boff + (c + 1) * 128, :], s_xg,
                         waits=scat_done + [xg_reader])
                gl = mark(s_xg)
                bkT = bank[id(pT)]
                for c in range(nct):
                    for dc in range(8):
                        p.emit("tensor", lambda e, c=c, dc=dc: e.transpose(
                            pT[:, 0:128], xg_st[:, c * D + dc * 128: c * D + (dc + 1) * 128], ident[:, :]),
                            waits=[gl, bkT["free"]], inc=pe)
                        tm = mark(pe)
                        p.emit("scalar", lambda e, c=c, dc=dc, bs_n=bs_n: e.copy(
                            xg_fm[:, dc * bs_n + c * 128: dc * bs_n + (c + 1) * 128], pT[:, 0:128]),
                            waits=[tm], inc=act, raw=True)
                        bkT["free"] = mark(act)
                xg_done = mark(act)
                xg_reader = xg_done
                for ic in range(32):
                    bk_t = hm_banks[ic % 2]
                    bk = bank[id(bk_t)]
                    for k in range(8):
                        p.emit("tensor", lambda e, ic=ic, k=k, bk_t=bk_t, bs_n=bs_n: e.matmul(
                            bk_t[:, 0:bs_n], w1_tile(k, ic), xg_fm[:, k * bs_n:(k + 1) * bs_n],
                            start=(k == 0), stop=(k == 7)),
                            waits=[wA_cur[0 if ic < 16 else 1], xg_done, bk["free"]], inc=pe)
                    if ic == 15:
                        w1a_mark = mark(pe)
                    mm = mark(pe)
                    p.emit("scalar", lambda e, ic=ic, bk_t=bk_t, bs_n=bs_n, par=par: e.activation(
                        hmid[:, ic * 256: ic * 256 + bs_n], bk_t[:, 0:bs_n], AF.Gelu,
                        bias=b1sb[:, par * 32 + ic: par * 32 + ic + 1]),
                        waits=[mm, b1_cur], inc=act, raw=True)
                    bk["free"] = mark(act)
                hm_done = mark(act)
                if bi == len(BLKS) - 1:
                    w1_gate = [mark(pe), mark(act)]
                    w1_gate_a = [w1a_mark, mark(act)]
                for st in range(nct):
                    hv_marks = []
                    for hh in range(2):
                        bk_t = ho_banks[(st * 2 + hh) % 4]
                        bk = bank[id(bk_t)]
                        for ic2 in range(32):
                            p.emit("tensor", lambda e, st=st, hh=hh, ic2=ic2, bk_t=bk_t: e.matmul(
                                bk_t[:, :], hmid[:, ic2 * 256 + st * 128: ic2 * 256 + (st + 1) * 128],
                                w2_tile(ic2, hh), start=(ic2 == 0), stop=False),
                                waits=[wB_cur[0 if ic2 < 16 else 1], hm_done, bk["free"]], inc=pe)
                            if ic2 == 15 and st == nct - 1 and hh == 1:
                                w2a_mark = mark(pe)
                        p.emit("tensor", lambda e, hh=hh, par=par, bk_t=bk_t: e.matmul(
                            bk_t[:, :], orow[:, :], b2sb[:, par * D + hh * 512: par * D + (hh + 1) * 512],
                            start=False, stop=True),
                            waits=[b2_cur], inc=pe)
                        mm = mark(pe)
                        hb = st % 2
                        p.emit("vector", lambda e, hb=hb, hh=hh, bk_t=bk_t: e.tensor_copy(
                            hout_buf[hb][:, hh * 512:(hh + 1) * 512], bk_t[:, :]),
                            waits=[mm, hout_reader[hb]], inc=dve, raw=True)
                        bk["free"] = mark(dve)
                    hv = mark(dve)
                    hb = st % 2
                    sdma("sync", d_ffn[soff + boff + st * 128: soff + boff + (st + 1) * 128, :],
                         hout_buf[hb][:, :], s_scd[hb], waits=[hv])
                    hout_reader[hb] = mark(s_scd[hb])
                deltas = [(s.h, s.n - sem_snap[id(s)]) for s in p.all_sems
                          if s.n > sem_snap[id(s)]]
                for ge_ in ("sync", "tensor", "scalar"):
                    if_close(ge_)
                if_close("vector", deltas=deltas)
                boff += bs_n
            if ei + 1 < E:
                load_w1(ei + 1, w1_gate_a, waits_b=w1_gate)
                load_w2(ei + 1, [mark(pe)], waits_a=[w2a_mark])
        ffn_done = [mark(s_scd[0]), mark(s_scd[1])]

    if upto >= 8:
        ep_ld = [f32a, f32b]
        ep_g = [f32c, f32d]
        ld_reader = [xt_reader[0], xt_reader[1]]
        g_reader = [hout_reader[0], hout_reader[1]]
        for t in range(NT):
            b = t % 2
            sdma("sync", ep_ld[b][:, :], d_base[t * 128:(t + 1) * 128, :], s_x[b],
                 waits=[ld_reader[b], base_stores[t]])
            bl = mark(s_x[b])
            p.emit("vector", lambda e, b=b: e.memset(ep_g[b][:, :], 0.0), waits=[g_reader[b]], inc=dve, raw=True)
            ms = mark(dve)
            p.emit("gpsimd", lambda e, t=t, b=b: e.indirect_dma_start(
                ep_g[b][:, :], None, d_ffn[0:NSLOT, :], IndirectOffsetOnAxis(slot[:, t:t + 1], 0),
                bounds_check=NSLOT - 1, oob_is_err=False),
                waits=[ms] + ffn_done, inc=s_gath[b], k=16)
            gm = mark(s_gath[b])
            p.emit("vector", lambda e, t=t, b=b: e.scalar_tensor_tensor(
                ep_ld[b][:, :], ep_g[b][:, :], gate[:, t:t + 1], ep_ld[b][:, :], ALU.mult, ALU.add),
                waits=[bl, gm], inc=dve, raw=True)
            om = mark(dve)
            g_reader[b] = om
            sdma("sync", dout[t * 128:(t + 1) * 128, :], ep_ld[b][:, :], s_sab[b], waits=[om])
            ld_reader[b] = mark(s_sab[b])

    if dbg:
        reg = {
            "xn_fm": (xn_fm, 128, 8 * S, BF16),
            "gate": (gate, 128, NT, F32),
            "idxf": (idxf, 128, NT, F32),
            "logit": (logit, 128, E, F32),
            "slot": (slot, 128, NT, I32),
            "c_q": (c_q, 128, 4 * S, BF16),
            "c_kv": (c_kv, 128, 4 * S, BF16),
            "kr12": (kr12, 32, S, BF16),
            "q96": (q96, 128, 8 * S, BF16),
            "qr12": (qr12, 128, 4 * S, BF16),
            "k96": (k96, 128, 8 * S, BF16),
            "v_a": (v_a, 128, 8 * 1040, BF16),
            "v_b": (v_b, 128, 8 * 1040, BF16),
            "attn_fm": (attn_fm, 128, 8 * S, BF16),
            "base_d": (d_base, S, D, F32),
            "ffn_d": (d_ffn, NSLOT, D, F32),
            "xg_d": (d_xg, NSLOT, D, BF16),
            "xn_tok": (d_xn, S, D, BF16),
        }
        allsems = [pe, dve, act, s_const, s_wr, s_w1, s_w2, s_w3, s_wo, s_wAa, s_wAb,
                   s_wBa, s_wBb, s_b1, s_b2, s_xg] + s_scat + s_x + s_xnb + s_sxnb +             s_scd + s_sab + s_gath
        allw = [mark(s) for s in allsems]
        for name in dbg:
            ten, rows, cols, dt = reg[name]
            dd = nc.dram_tensor("dbg_" + name, [rows, cols], dt, kind="ExternalOutput").ap()
            sdma("sync", dd[:, :], ten[0:rows, 0:cols], s_sab[0], waits=allw)

    finsems = [s_sab[0], s_sab[1], s_scd[0], s_scd[1], s_gath[0], s_gath[1]] + s_scat
    p.emit("sync", lambda e: None, waits=[mark(s) for s in finsems])

    with nc.Block() as block:
        @block.sync
        def _(eng):
            for f in p.streams["sync"]:
                f(eng)

        @block.gpsimd
        def _(eng):
            for f in p.streams["gpsimd"]:
                f(eng)

        @block.tensor
        def _(eng):
            for f in p.streams["tensor"]:
                f(eng)

        @block.vector
        def _(eng):
            for f in p.streams["vector"]:
                f(eng)

        @block.scalar
        def _(eng):
            for f in p.streams["scalar"]:
                f(eng)

    return nc


def _host_weights(inp):
    o = {}
    o["wdq_t"] = np.ascontiguousarray(inp["L0_attn_W_DQ"].T)
    o["wdkv_t"] = np.ascontiguousarray(inp["L0_attn_W_DKV"].T)
    o["wuq_t"] = np.ascontiguousarray(inp["L0_attn_W_UQ"].T)
    o["wuk_t"] = np.ascontiguousarray(inp["L0_attn_W_UK"].T)
    o["wuv_t"] = np.ascontiguousarray(inp["L0_attn_W_UV"].T)
    o["wo_t"] = np.ascontiguousarray(inp["L0_attn_W_O"].T)
    wqr_t = inp["L0_attn_W_QR"].T
    q1 = np.zeros((DLAT, 512), np.float32)
    q2 = np.zeros((DLAT, 512), np.float32)
    q1s = np.zeros((DLAT, 512), np.float32)
    q2s = np.zeros((DLAT, 512), np.float32)
    for h in range(H):
        c4, rb4 = rope_slot(h)
        co = c4 * 128 + rb4
        q1[:, co:co + 16] = wqr_t[:, 32 * h + 0:32 * h + 32:2]
        q2[:, co:co + 16] = wqr_t[:, 32 * h + 1:32 * h + 32:2]
        q1s[:, co + 16:co + 32] = wqr_t[:, 32 * h + 0:32 * h + 32:2]
        q2s[:, co + 16:co + 32] = wqr_t[:, 32 * h + 1:32 * h + 32:2]
    o["wqr1"], o["wqr2"], o["wqr1s"], o["wqr2s"] = q1, q2, q1s, q2s
    wkr_t = inp["L0_attn_W_KR"].T
    kp = np.zeros((D, 64), np.float32)
    kps = np.zeros((D, 64), np.float32)
    kp[:, 0:16] = wkr_t[:, 0::2]
    kp[:, 32:48] = wkr_t[:, 1::2]
    kps[:, 16:32] = wkr_t[:, 1::2]
    kps[:, 48:64] = wkr_t[:, 0::2]
    o["wkrp"], o["wkrps"] = kp, kps
    o["wr"] = np.ascontiguousarray(inp["L0_ffn_wr"])
    o["br"] = np.ascontiguousarray(inp["L0_ffn_br"].reshape(1, E))
    o["w1"] = np.ascontiguousarray(inp["L0_ffn_w1"])
    o["b1t"] = np.ascontiguousarray(inp["L0_ffn_b1"].reshape(E, 32, 128).transpose(0, 2, 1))
    o["w2"] = np.ascontiguousarray(inp["L0_ffn_w2"])
    o["b2"] = np.ascontiguousarray(inp["L0_ffn_b2"])
    return o


def kernel(**inputs):
    x = inputs["L0_attn_x"]
    w = _host_weights(inputs)
    nc = build_core()
    in_maps = []
    for b in range(8):
        m = {"x": np.ascontiguousarray(x[b])}
        m.update(w)
        in_maps.append(m)
    res = run_bass_kernel_spmd(nc, in_maps, core_ids=list(range(8)))
    return np.stack([r["out"] for r in res.results], axis=0)


if __name__ == "__main__":
    nc = build_core()
    print("built ok; instructions:", nc.n_instructions())
    print("sbuf remaining:", nc.sbuf_bytes_remaining)


# version 17
# speedup vs baseline: 1.1278x; 1.1278x over previous
import math
import numpy as np
import ml_dtypes
import concourse.bass as bass
import concourse.mybir as mybir
from concourse.bass import IndirectOffsetOnAxis
from concourse.bass_utils import run_bass_kernel_spmd

F32 = mybir.dt.float32
BF16 = mybir.dt.bfloat16
I32 = mybir.dt.int32
AF = mybir.ActivationFunctionType
ALU = mybir.AluOpType
AX = mybir.AxisListType.X

S = 2048
D = 1024
H = 16
DHR = 32
DLAT = 512
E = 8
DI = 4096
CAP = 1152
NSLOT = E * CAP
NT = S // 128
SCALE = 1.0 / math.sqrt(96.0)
CLO = math.exp(-80.0 * SCALE)
CHI = math.exp(80.0 * SCALE)
BLKS = [256, 256, 256, 256, 128]

ENGINES = ("sync", "gpsimd", "tensor", "vector", "scalar")


def rope_slot(h):
    return h // 4, 32 * (h % 4)


class SemC:
    __slots__ = ("h", "n", "name", "is_dma")

    def __init__(self, h, name):
        self.h = h
        self.n = 0
        self.name = name
        self.is_dma = False


class Prog:
    def __init__(self, nc):
        self.nc = nc
        self.streams = {e: [] for e in ENGINES}
        self.waited = {}
        self.eng_sems = {}
        self.all_sems = []

    def sem(self, name):
        s = SemC(self.nc.alloc_semaphore(name), name)
        self.all_sems.append(s)
        return s

    def emit(self, eng, build, waits=(), inc=None, k=1, raw=False):
        if raw and eng in self.eng_sems:
            s = self.eng_sems[eng]
            waits = list(waits) + [(s, s.n)]
        wl = []
        for w in waits:
            if w is None:
                continue
            s, v = (w if isinstance(w, tuple) else (w, w.n))
            if v <= 0:
                continue
            key = (eng, id(s))
            if self.waited.get(key, -1) < v:
                self.waited[key] = v
                wl.append((s.h, v))
        if inc is not None:
            inc.n += k
        hk = inc.h if inc is not None else None

        def th(e, build=build, wl=wl, hk=hk, k=k):
            for (h, v) in wl:
                e.wait_ge(h, v)
            ins = build(e)
            if hk is not None and ins is not None:
                ins.then_inc(hk, k)
        self.streams[eng].append(th)


def _cosf_sinf():
    inv = 10000.0 ** (-np.arange(0, DHR, 2, dtype=np.float64) / DHR)
    pos = np.arange(S, dtype=np.float64)
    ang = pos[None, :] * inv[:, None]
    cf = np.tile(np.cos(ang), (8, 1)).astype(ml_dtypes.bfloat16)
    sf = np.tile(np.sin(ang), (8, 1)).astype(ml_dtypes.bfloat16)
    return cf, sf


def _mtab():
    kp = np.arange(128)[:, None]
    c = np.arange(896)[None, :]
    return (kp <= c - 384).astype(ml_dtypes.bfloat16)


def build_core(upto=99, dbg=None):
    nc = bass.Bass("TRN2", target_bir_lowering=False)
    p = Prog(nc)

    dx = nc.dram_tensor("x", [S, D], F32, kind="ExternalInput").ap()
    dWDQ = nc.dram_tensor("wdq_t", [D, DLAT], F32, kind="ExternalInput").ap()
    dWDKV = nc.dram_tensor("wdkv_t", [D, DLAT], F32, kind="ExternalInput").ap()
    dWUQ = nc.dram_tensor("wuq_t", [DLAT, D], F32, kind="ExternalInput").ap()
    dWUK = nc.dram_tensor("wuk_t", [DLAT, D], F32, kind="ExternalInput").ap()
    dWUV = nc.dram_tensor("wuv_t", [DLAT, D], F32, kind="ExternalInput").ap()
    dWQ1 = nc.dram_tensor("wqr1", [DLAT, 512], F32, kind="ExternalInput").ap()
    dWQ2 = nc.dram_tensor("wqr2", [DLAT, 512], F32, kind="ExternalInput").ap()
    dWQ1s = nc.dram_tensor("wqr1s", [DLAT, 512], F32, kind="ExternalInput").ap()
    dWQ2s = nc.dram_tensor("wqr2s", [DLAT, 512], F32, kind="ExternalInput").ap()
    dWKR = nc.dram_tensor("wkrp", [D, 64], F32, kind="ExternalInput").ap()
    dWKRs = nc.dram_tensor("wkrps", [D, 64], F32, kind="ExternalInput").ap()
    dWO = nc.dram_tensor("wo_t", [D, D], F32, kind="ExternalInput").ap()
    dwr = nc.dram_tensor("wr", [D, E], F32, kind="ExternalInput").ap()
    dbr = nc.dram_tensor("br", [1, E], F32, kind="ExternalInput").ap()
    dw1 = nc.dram_tensor("w1", [E, D, DI], F32, kind="ExternalInput").ap()
    db1 = nc.dram_tensor("b1t", [E, 128, 32], F32, kind="ExternalInput").ap()
    dw2 = nc.dram_tensor("w2", [E, DI, D], F32, kind="ExternalInput").ap()
    db2 = nc.dram_tensor("b2", [E, D], F32, kind="ExternalInput").ap()
    dout = nc.dram_tensor("out", [S, D], F32, kind="ExternalOutput").ap()

    d_xn = nc.dram_tensor("xn_tok", [S, D], BF16, kind="Internal").ap()
    d_xg = nc.dram_tensor("xg", [NSLOT, D], BF16, kind="Internal").ap()
    d_ffn = nc.dram_tensor("ffn", [NSLOT, D], F32, kind="Internal").ap()
    d_base = nc.dram_tensor("base", [S, D], F32, kind="Internal").ap()

    cf_np, sf_np = _cosf_sinf()
    d_cf = nc.inline_tensor(np.ascontiguousarray(cf_np), "c_cf").ap()
    d_sf = nc.inline_tensor(np.ascontiguousarray(sf_np), "c_sf").ap()
    d_m = nc.inline_tensor(np.ascontiguousarray(_mtab()), "c_mtab").ap()
    d_id = nc.inline_tensor(np.eye(128, dtype=ml_dtypes.bfloat16), "c_ident").ap()
    ut = (np.arange(128)[:, None] < np.arange(128)[None, :]).astype(ml_dtypes.bfloat16)
    d_ut = nc.inline_tensor(ut, "c_utri").ap()
    d_on = nc.inline_tensor(np.ones((128, 128), dtype=ml_dtypes.bfloat16), "c_ones").ap()
    d_or = nc.inline_tensor(np.ones((1, 128), dtype=ml_dtypes.bfloat16), "c_onesrow").ap()
    d_orf = nc.inline_tensor(np.ones((1, 64), dtype=np.float32), "c_onesrowf").ap()
    iota_np = np.broadcast_to(np.arange(E, dtype=np.float32), (128, E)).copy()
    d_io = nc.inline_tensor(iota_np, "c_iota8").ap()

    TOP = nc.sbuf_top
    KB = 1024
    MAPKB = 207.0
    BASE0 = TOP - int(MAPKB * KB)
    assert BASE0 >= nc.sbuf_base, (BASE0, nc.sbuf_base)
    _placed = []

    def at(name, shape, dtype, off_kb):
        esz = {F32: 4, BF16: 2, I32: 4}[dtype]
        nbytes = int(np.prod(shape[1:])) * esz
        off = BASE0 + int(off_kb * KB)
        assert off + nbytes <= TOP, (name, off_kb, nbytes)
        _placed.append((name, off, off + nbytes))
        return nc.alloc_sbuf_tensor_at(name, list(shape), dtype, offset=off)

    f32a = at("f32a", [128, D], F32, 0)
    f32b = at("f32b", [128, D], F32, 4)
    ident = at("ident", [128, 128], BF16, 8)
    utri = at("utri", [128, 128], BF16, 8.25)
    ones = at("ones", [128, 128], BF16, 8.5)
    orow = at("orow", [1, 128], BF16, 8.75)
    orowf = at("orowf", [1, 64], F32, 9.0)
    iot = at("iota8", [128, E], F32, 9.25)
    logit = at("logit", [128, E], F32, 9.5)
    s8a = at("s8a", [128, E], F32, 9.5625)
    s8b = at("s8b", [128, E], F32, 9.625)
    m1t = at("m1t", [128, 16], F32, 9.6875)
    oh_f = at("oh_f", [128, NT * E], F32, 9.75)
    oh_b = at("oh_b", [128, NT * E], BF16, 10.25)
    gate = at("gate", [128, NT], F32, 10.5)
    idxf = at("idxf", [128, NT], F32, 10.5625)
    slot = at("slot", [128, NT], I32, 10.625)
    rankf = at("rankf", [128, E], F32, 10.6875)
    sltf = at("sltf", [128, 4], F32, 10.75)
    wrsb = at("wrsb", [128, E * 8], BF16, 10.8125)
    brsb = at("brsb", [1, E], BF16, 10.9375)
    cntf = at("cntf", [1, E], F32, 11.0)
    flgf = at("flgf", [1, 64], F32, 11.0625)
    flgi = at("flgi", [1, 64], I32, 11.3125)
    junk = at("junk", [1, 4], F32, 11.5625)
    kr12 = at("kr12", [32, S], BF16, 11.625)
    EB = 16.0
    xn_fm = at("xn_fm", [128, 8 * S], BF16, EB)
    ropeq = [at(f"ropeq{i}", [128, S], BF16, EB + 4 * i) for i in range(4)]
    ropet = [at(f"ropet{i}", [128, S], BF16, EB + 16 + 4 * i) for i in range(4)]
    attn_fm = at("attn_fm", [128, 8 * S], BF16, EB)
    c_q = at("c_q", [128, 4 * S], BF16, EB + 32)
    c_kv = at("c_kv", [128, 4 * S], BF16, EB + 48)
    q96 = at("q96", [128, 8 * S], BF16, EB + 64)
    wbufP1 = at("wbufP1", [128, 4096], BF16, EB + 64)
    kpre = at("kpre", [64, S], BF16, EB + 72)
    kpres = at("kpres", [64, S], BF16, EB + 76)
    ktmp = at("ktmp", [32, S], BF16, EB + 80)
    ktmp2 = at("ktmp2", [32, S], BF16, EB + 84)
    xnb0 = at("xnb0", [128, D], BF16, EB + 88)
    xnb1 = at("xnb1", [128, D], BF16, EB + 90)
    k96 = at("k96", [128, 8 * S], BF16, EB + 96)
    wbufP2 = at("wbufP2", [128, 4096], BF16, EB + 96)
    wbufP3 = at("wbufP3", [128, 4096], BF16, EB + 104)
    wobuf = at("wobuf", [128, 8 * D], BF16, EB + 96)
    cf = at("cf", [128, S], BF16, EB + 128)
    sf = at("sf", [128, S], BF16, EB + 132)
    v_a = at("v_a", [128, 8 * 1040], BF16, EB + 128)
    v_b = at("v_b", [128, 8 * 1040], BF16, EB + 144.25)
    rnka = at("rnka", [128, NT * E], F32, EB + 144.25)
    rsel = at("rsel", [128, NT], F32, EB + 144.75)
    ovr16 = at("ovr16", [128, NT], F32, EB + 144.8125)
    slf16 = at("slf16", [128, NT], F32, EB + 144.875)
    qr12 = at("qr12", [128, 4 * S], BF16, EB + 160.5)
    mtab = at("mtab", [128, 896], BF16, EB + 176.5)
    rcp = at("rcp", [1, 512], F32, EB + 178.5)
    rbsb = at("rbsb", [64, 512], F32, EB + 180.5)
    exps_a = at("exps_a", [128, 4 * 512], BF16, EB + 182.5)
    exps_b = at("exps_b", [128, 4 * 512], BF16, EB + 186.5)
    wbufQ = at("wbufQ", [128, 4096], BF16, EB + 182.5)
    wAa = at("wAa", [128, 16384], BF16, EB + 64)
    wAb = at("wAb", [128, 16384], BF16, EB + 96)
    wBa = at("wBa", [128, 16384], BF16, EB)
    wBb = at("wBb", [128, 16384], BF16, EB + 32)
    hmid = at("hmid", [128, 32 * 256], BF16, EB + 128)
    xg_st = at("xg_st", [128, 2 * D], BF16, EB + 144)
    xg_fm = at("xg_fm", [128, 8 * 256], BF16, EB + 148)
    b1sb = at("b1sb", [128, 64], F32, EB + 152)
    b2sb = at("b2sb", [1, 2 * D], BF16, EB + 152.25)
    f32c = at("f32c", [128, D], F32, EB + 176.5)
    f32d = at("f32d", [128, D], F32, EB + 180.5)

    PS4 = nc.alloc_psum_tensor("PS4", [128, 2048], F32)
    pA = PS4[:, 0:512]
    pB = PS4[:, 512:1024]
    pC = PS4[:, 1024:1536]
    pD = PS4[:, 1536:2048]
    pE = nc.alloc_psum_tensor("pE", [128, 512], F32)
    pF = nc.alloc_psum_tensor("pF", [128, 512], F32)
    pG = nc.alloc_psum_tensor("pG", [128, 512], F32)
    pT = nc.alloc_psum_tensor("pT", [128, 1024], BF16)

    s_const = p.sem("s_const")
    s_wr = p.sem("s_wr")
    s_x = [p.sem("s_x0"), p.sem("s_x1")]
    s_xnb = [p.sem("s_xnb0"), p.sem("s_xnb1")]
    s_sxnb = [p.sem("s_sxnb0"), p.sem("s_sxnb1")]
    s_scd = [p.sem("s_scd0"), p.sem("s_scd1")]
    s_sab = [p.sem("s_sab0"), p.sem("s_sab1")]
    s_w1 = p.sem("s_w1")
    s_w2 = p.sem("s_w2")
    s_w3 = p.sem("s_w3")
    s_wo = p.sem("s_wo")
    s_wAa = p.sem("s_wAa")
    s_wAb = p.sem("s_wAb")
    s_wBa = p.sem("s_wBa")
    s_wBb = p.sem("s_wBb")
    s_b1 = p.sem("s_b1")
    s_b2 = p.sem("s_b2")
    s_scat = [p.sem("s_scat0"), p.sem("s_scat1")]
    s_gath = [p.sem("s_gath0"), p.sem("s_gath1")]
    s_zero = p.sem("s_zero")
    s_wq = p.sem("s_wq")
    s_pk = p.sem("s_pk")
    s_xg = p.sem("s_xg")
    wbuf_sem = {}
    pe = p.sem("pe")
    act = p.sem("act")
    dve = p.sem("dve")
    p.eng_sems = {"vector": dve, "scalar": act}

    def sdma(eng, out_ap, in_ap, sem, waits=()):
        p.emit(eng, lambda e, o=out_ap, i=in_ap: e.dma_start(out=o, in_=i),
               waits=waits, inc=sem, k=16)

    def mark(sem):
        return (sem, sem.n)

    wbuf_sem.update({id(wbufP1): s_w1, id(wbufP2): s_w2, id(wbufP3): s_w3, id(wobuf): s_wo})

    scA = PS4[:, 0:1024]
    scB = PS4[:, 1024:2048]
    bank = {id(b): {"free": None} for b in (pA, pB, pC, pD, pE, pF, pG, pT, scA, scB)}

    def load_w(buf, dram_ap, ncols, nchunk, extra=(), conservative=True):
        sem = wbuf_sem[id(buf)]
        w = ([mark(pe), mark(dve)] if conservative else []) + list(extra)
        sdma("gpsimd", buf[:, 0:ncols].rearrange("p (c o) -> p c o", c=nchunk), dram_ap,
             sem, waits=w)
        return mark(sem)

    sdma("sync", cf[:, :], d_cf[:, :], s_const)
    sdma("sync", sf[:, :], d_sf[:, :], s_const)
    sdma("sync", ident[:, :], d_id[:, :], s_const)
    sdma("sync", utri[:, :], d_ut[:, :], s_const)
    sdma("sync", ones[:, :], d_on[:, :], s_const)
    sdma("sync", orow[:, :], d_or[:, :], s_const)
    sdma("sync", orowf[:, :], d_orf[:, :], s_const)
    sdma("sync", iot[:, :], d_io[:, :], s_const)
    sdma("gpsimd", wrsb[:, :].rearrange("p (c e) -> p c e", c=8), dwr.rearrange("(c p) e -> p c e", p=128), s_wr)
    sdma("gpsimd", brsb[:, :], dbr[:, :], s_wr)
    consts_ld = mark(s_const)
    wr_ld = mark(s_wr)
    pool = p.sem("pool")
    p.emit("gpsimd", lambda e: e.memset(xnb0[:, :], 0.0), inc=pool)
    zmemset = mark(pool)
    for zi in range(NSLOT // 128):
        sdma("gpsimd", d_xg[zi * 128:(zi + 1) * 128, :], xnb0[:, :], s_zero, waits=[zmemset])
    zero_done = mark(s_zero)

    xt = [f32a, f32b]
    xnb = [xnb0, xnb1]
    xt_reader = [None, None]
    xnb_reader = [[zero_done], None]
    xn_store = []
    oh_done = []
    pa_free = None

    if upto >= 1:
        for t in range(NT):
            b = t % 2
            sdma("sync", xt[b][:, :], dx[t * 128:(t + 1) * 128, :], s_x[b],
                 waits=[xt_reader[b]])
            ldm = mark(s_x[b])
            Q = (t % 2) * 8
            p.emit("vector", lambda e, b=b, Q=Q: e.reduce_sum(m1t[:, Q:Q + 1], xt[b][:, :], axis=AX),
                   waits=[ldm], inc=dve, raw=True)
            p.emit("vector", lambda e, Q=Q: e.tensor_scalar_mul(m1t[:, Q + 1:Q + 2], m1t[:, Q:Q + 1],
                                                                1.0 / D), inc=dve, raw=True)
            p.emit("vector", lambda e, b=b, Q=Q: e.tensor_scalar(xt[b][:, :], xt[b][:, :],
                                                                 m1t[:, Q + 1:Q + 2], None,
                                                                 ALU.subtract), inc=dve, raw=True)
            cent = mark(dve)
            p.emit("scalar", lambda e, b=b, Q=Q: e.activation(xnb[b][:, :], xt[b][:, :], AF.Square,
                                                              accum_out=m1t[:, Q + 2:Q + 3]),
                   waits=[cent] + (xnb_reader[b] or []), inc=act, raw=True)
            sq = mark(act)
            p.emit("vector", lambda e, Q=Q: e.tensor_scalar(m1t[:, Q + 3:Q + 4], m1t[:, Q + 2:Q + 3],
                                                            1.0 / D, 1e-5,
                                                            ALU.mult, ALU.add), waits=[sq], inc=dve, raw=True)
            vpe = mark(dve)
            p.emit("scalar", lambda e, Q=Q: e.activation(m1t[:, Q + 1:Q + 2], m1t[:, Q + 3:Q + 4], AF.Ln),
                   waits=[vpe], inc=act, raw=True)
            p.emit("scalar", lambda e, Q=Q: e.activation(m1t[:, Q + 2:Q + 3], m1t[:, Q + 1:Q + 2],
                                                         AF.Exp, scale=-0.5), inc=act, raw=True)
            rstd = mark(act)
            p.emit("vector", lambda e, b=b, Q=Q: e.tensor_scalar(xnb[b][:, :], xt[b][:, :],
                                                                 m1t[:, Q + 2:Q + 3], None,
                                                                 ALU.mult), waits=[rstd], inc=dve, raw=True)
            xnm = mark(dve)
            xt_reader[b] = xnm
            sdma("sync", d_xn[t * 128:(t + 1) * 128, :], xnb[b][:, :], s_sxnb[b], waits=[xnm])
            xn_store.append(mark(s_sxnb[b]))
            for c in range(8):
                tb = pT[:, 0:128] if c % 2 == 0 else pC[:, 0:64].bitcast(BF16)
                tkey = id(pT) if c % 2 == 0 else id(pC)
                p.emit("tensor", lambda e, b=b, c=c, tb=tb: e.transpose(tb, xnb[b][:, c * 128:(c + 1) * 128],
                                                                        ident[:, :]),
                       waits=[xnm, consts_ld, bank[tkey]["free"]], inc=pe)
                tm = mark(pe)
                if c % 2 == 0:
                    p.emit("scalar", lambda e, c=c, t=t, tb=tb: e.copy(
                        xn_fm[:, c * S + t * 128: c * S + (t + 1) * 128], tb),
                        waits=[tm], inc=act, raw=True)
                    bank[tkey]["free"] = mark(act)
                else:
                    p.emit("vector", lambda e, c=c, t=t, tb=tb: e.tensor_copy(
                        xn_fm[:, c * S + t * 128: c * S + (t + 1) * 128], tb),
                        waits=[tm], inc=dve, raw=True)
                    bank[tkey]["free"] = mark(dve)
            xnb_reader[b] = [mark(act), mark(dve), mark(s_sxnb[b])]
            xnfm_t = [mark(act), mark(dve)]
            for c in range(8):
                p.emit("tensor", lambda e, c=c, t=t: e.matmul(pA[:, 0:E],
                                                              xn_fm[:, c * S + t * 128: c * S + (t + 1) * 128],
                                                              wrsb[:, c * E:(c + 1) * E],
                                                              start=(c == 0), stop=False),
                       waits=xnfm_t + [wr_ld, pa_free], inc=pe)
            p.emit("tensor", lambda e: e.matmul(pA[:, 0:E], orow[:, :], brsb[:, :], start=False, stop=True),
                   waits=[consts_ld], inc=pe)
            rmm = mark(pe)
            p.emit("vector", lambda e: e.tensor_copy(logit[:, :], pA[:, 0:E]), waits=[rmm], inc=dve, raw=True)
            pa_free = mark(dve)
            p.emit("vector", lambda e, Q=Q: e.reduce_max(m1t[:, Q + 4:Q + 5], logit[:, :], axis=AX), inc=dve, raw=True)
            p.emit("vector", lambda e, t=t, Q=Q: e.tensor_scalar(oh_f[:, t * E:(t + 1) * E], logit[:, :],
                                                                 m1t[:, Q + 4:Q + 5], 0.0,
                                                                 ALU.subtract, ALU.is_ge), inc=dve, raw=True)
            p.emit("vector", lambda e, t=t: e.tensor_copy(oh_b[:, t * E:(t + 1) * E], oh_f[:, t * E:(t + 1) * E]),
                   inc=dve, raw=True)
            p.emit("vector", lambda e, t=t: e.tensor_tensor(s8a[:, :], oh_f[:, t * E:(t + 1) * E], iot[:, :],
                                                            ALU.mult), waits=[consts_ld], inc=dve, raw=True)
            p.emit("vector", lambda e, t=t: e.reduce_sum(idxf[:, t:t + 1], s8a[:, :], axis=AX), inc=dve, raw=True)
            p.emit("vector", lambda e, t=t: e.scalar_tensor_tensor(s8b[:, :], oh_f[:, t * E:(t + 1) * E], -1e30,
                                                                   logit[:, :], ALU.mult, ALU.add), inc=dve, raw=True)
            p.emit("vector", lambda e, Q=Q: e.reduce_max(m1t[:, Q + 5:Q + 6], s8b[:, :], axis=AX), inc=dve, raw=True)
            p.emit("vector", lambda e, Q=Q: e.tensor_tensor(m1t[:, Q + 6:Q + 7], m1t[:, Q + 4:Q + 5],
                                                            m1t[:, Q + 5:Q + 6], ALU.subtract),
                   inc=dve, raw=True)
            dl = mark(dve)
            p.emit("scalar", lambda e, Q=Q: e.activation(m1t[:, Q + 7:Q + 8], m1t[:, Q + 6:Q + 7],
                                                         AF.Exp, scale=-1.0),
                   waits=[dl], inc=act, raw=True)
            ge = mark(act)
            p.emit("vector", lambda e, Q=Q: e.tensor_scalar(m1t[:, Q + 6:Q + 7], m1t[:, Q + 7:Q + 8],
                                                            1.0, None, ALU.add),
                   waits=[ge], inc=dve, raw=True)
            p.emit("vector", lambda e, t=t, Q=Q: e.reciprocal(gate[:, t:t + 1], m1t[:, Q + 6:Q + 7]), inc=dve, raw=True)
            oh_done.append(mark(dve))

    scat_done = None
    if upto >= 2:
        allo = oh_done[-1]
        pb_free = None
        for t in range(NT):
            p.emit("tensor", lambda e, t=t: e.matmul(pB[:, 0:E], utri[:, :], oh_b[:, t * E:(t + 1) * E],
                                                     start=True, stop=(t == 0)),
                   waits=[allo, consts_ld, pb_free], inc=pe)
            for tp in range(t):
                p.emit("tensor", lambda e, tp=tp, t=t: e.matmul(pB[:, 0:E], ones[:, :],
                                                                oh_b[:, tp * E:(tp + 1) * E],
                                                                start=False, stop=(tp == t - 1)), inc=pe)
            rk = mark(pe)
            p.emit("vector", lambda e, t=t: e.tensor_copy(rnka[:, t * E:(t + 1) * E], pB[:, 0:E]),
                   waits=[rk], inc=dve, raw=True)
            pb_free = mark(dve)
        p.emit("vector", lambda e: e.tensor_tensor(oh_f[:, :], rnka[:, :], oh_f[:, :], ALU.mult),
               inc=dve, raw=True)
        p.emit("vector", lambda e: e.reduce_sum(rsel[:, :],
                                                oh_f[:, :].rearrange("p (t e) -> p t e", e=E),
                                                axis=AX), inc=dve, raw=True)
        p.emit("vector", lambda e: e.scalar_tensor_tensor(slf16[:, :], idxf[:, :], float(CAP),
                                                          rsel[:, :], ALU.mult, ALU.add), inc=dve, raw=True)
        p.emit("vector", lambda e: e.tensor_scalar(ovr16[:, :], rsel[:, :], float(CAP) - 0.5, 0.0,
                                                   ALU.subtract, ALU.is_ge), inc=dve, raw=True)
        p.emit("vector", lambda e: e.scalar_tensor_tensor(slf16[:, :], ovr16[:, :], 1e7,
                                                          slf16[:, :], ALU.mult, ALU.add), inc=dve, raw=True)
        p.emit("vector", lambda e: e.tensor_copy(slot[:, :], slf16[:, :]), inc=dve, raw=True)
        sl = mark(dve)
        for t in range(NT):
            b = t % 2
            sdma("sync", xnb[b][:, :], d_xn[t * 128:(t + 1) * 128, :], s_xnb[b],
                 waits=[xn_store[t]] + (xnb_reader[b] or []))
            xr = mark(s_xnb[b])
            p.emit("gpsimd", lambda e, t=t, b=b: e.indirect_dma_start(
                d_xg[0:NSLOT, :], IndirectOffsetOnAxis(slot[:, t:t + 1], 0),
                xnb[b][:, :], None,
                bounds_check=NSLOT - 1, oob_is_err=False),
                waits=[sl, xr, zero_done], inc=s_scat[b], k=16)
            xnb_reader[b] = [mark(s_scat[b])]
        scat_done = [mark(s_scat[0]), mark(s_scat[1])]
        for t in range(NT):
            p.emit("tensor", lambda e, t=t: e.matmul(pC[0:1, 0:E], ones[:, 0:1],
                                                     oh_b[:, t * E:(t + 1) * E],
                                                     start=(t == 0), stop=(t == NT - 1)),
                   waits=[allo, consts_ld], inc=pe)
        cmm = mark(pe)
        p.emit("vector", lambda e: e.tensor_copy(cntf[:, :], pC[0:1, 0:E]), waits=[cmm], inc=dve, raw=True)
        bank[id(pC)]["free"] = mark(dve)
        boff_acc = 0
        for b5 in range(len(BLKS)):
            p.emit("vector", lambda e, b5=b5, bo=boff_acc: e.tensor_scalar(
                flgf[0:1, b5 * E:(b5 + 1) * E], cntf[0:1, :], bo + 0.5, 0.0,
                ALU.subtract, ALU.is_ge), inc=dve, raw=True)
            boff_acc += BLKS[b5]
        p.emit("vector", lambda e: e.tensor_copy(flgi[0:1, 0:len(BLKS) * E], flgf[0:1, 0:len(BLKS) * E]),
               inc=dve, raw=True)
        flag_gate = mark(dve)

    def proj_fm(dst, w_dram, wbuf, kdim, odim, rhs, rhs_ready, psums, dve_out=True, first=False):
        kc = kdim // 128
        oc = odim // 128
        wl = load_w(wbuf, w_dram.rearrange("(c p) o -> p c o", p=128), kc * odim, kc,
                    conservative=not first)
        done = None
        for o in range(oc):
            for tt in range(4):
                bk_t = psums[(o * 4 + tt) % len(psums)]
                bk = bank[id(bk_t)]
                for k in range(kc):
                    p.emit("tensor", lambda e, o=o, tt=tt, k=k, bk_t=bk_t, wbuf=wbuf, odim=odim, kc=kc, rhs=rhs:
                           e.matmul(bk_t[:, :],
                                    wbuf[:, k * odim + o * 128: k * odim + (o + 1) * 128],
                                    rhs[:, k * S + tt * 512: k * S + (tt + 1) * 512],
                                    start=(k == 0), stop=(k == kc - 1)),
                           waits=[wl, bk["free"]] + (rhs_ready if isinstance(rhs_ready, list) else [rhs_ready]), inc=pe)
                mm = mark(pe)
                p.emit("vector", lambda e, o=o, tt=tt, bk_t=bk_t, dst=dst: e.tensor_copy(
                    dst[:, o * S + tt * 512: o * S + (tt + 1) * 512], bk_t[:, :]),
                    waits=[mm], inc=dve, raw=True)
                bk["free"] = mark(dve)
        return mark(dve)

    if upto >= 3:
        xn_last = mark(act)
        xn_last_d = mark(dve)
        cq_done = proj_fm(c_q, dWDQ, wbufP1, D, DLAT, xn_fm, [xn_last, xn_last_d], (pA, pB),
                          first=True)
        ckv_done = proj_fm(c_kv, dWDKV, wbufP1, D, DLAT, xn_fm, [xn_last, xn_last_d], (pA, pB))
        for (wd, dst) in ((dWKR, kpre), (dWKRs, kpres)):
            wl = load_w(wbufP1, wd.rearrange("(c p) o -> p c o", p=128), 8 * 64, 8)
            for tt in range(4):
                bk = bank[id(pC)]
                for k in range(8):
                    p.emit("tensor", lambda e, tt=tt, k=k: e.matmul(
                        pC[0:64, :], wbufP1[:, k * 64:(k + 1) * 64],
                        xn_fm[:, k * S + tt * 512: k * S + (tt + 1) * 512],
                        start=(k == 0), stop=(k == 7)),
                        waits=[wl, xn_last, xn_last_d, bk["free"]], inc=pe)
                mm = mark(pe)
                p.emit("vector", lambda e, dst=dst, tt=tt: e.tensor_copy(dst[:, tt * 512:(tt + 1) * 512],
                                                                         pC[0:64, :]),
                       waits=[mm], inc=dve, raw=True)
                bk["free"] = mark(dve)
        kp_done = mark(dve)
        p.emit("vector", lambda e: e.tensor_tensor(ktmp[:, :], kpre[0:32, :], cf[0:32, :], ALU.mult),
               waits=[kp_done, consts_ld], inc=dve, raw=True)
        p.emit("vector", lambda e: e.tensor_tensor(ktmp2[:, :], kpre[32:64, :], sf[32:64, :], ALU.mult), inc=dve, raw=True)
        p.emit("vector", lambda e: e.tensor_tensor(ktmp[:, :], ktmp[:, :], ktmp2[:, :], ALU.subtract), inc=dve, raw=True)
        p.emit("vector", lambda e: e.tensor_tensor(ktmp2[:, :], kpres[32:64, :], sf[32:64, :], ALU.mult), inc=dve, raw=True)
        p.emit("vector", lambda e: e.tensor_tensor(kr12[0:32, :], kpres[0:32, :], cf[0:32, :], ALU.mult), inc=dve, raw=True)
        p.emit("vector", lambda e: e.tensor_tensor(ktmp2[:, :], ktmp2[:, :], kr12[0:32, :], ALU.add), inc=dve, raw=True)
        p.emit("vector", lambda e: e.tensor_tensor(kr12[0:32, :], ktmp[:, :], ktmp2[:, :], ALU.add), inc=dve, raw=True)
        kr_done = mark(dve)

    if upto >= 4:
        rope_cnt = [0]
        rope_buf_reader = {id(wbufP2): None, id(wbufP3): None}

        def rope_proj(dst_list, w_dram):
            wb = (wbufP2, wbufP3)[rope_cnt[0] % 2]
            rope_cnt[0] += 1
            wl = load_w(wb, w_dram.rearrange("(c p) o -> p c o", p=128), 4 * 512, 4,
                        conservative=False, extra=[rope_buf_reader[id(wb)]])
            for c in range(4):
                for tt in range(4):
                    bk_t = (pA, pB)[tt % 2]
                    bk = bank[id(bk_t)]
                    for k in range(4):
                        p.emit("tensor", lambda e, c=c, tt=tt, k=k, bk_t=bk_t, wb=wb: e.matmul(
                            bk_t[:, :], wb[:, k * 512 + c * 128: k * 512 + (c + 1) * 128],
                            c_q[:, k * S + tt * 512: k * S + (tt + 1) * 512],
                            start=(k == 0), stop=(k == 3)),
                            waits=[wl, cq_done, bk["free"]], inc=pe)
                    mm = mark(pe)
                    p.emit("vector", lambda e, dst=dst_list[c], tt=tt, bk_t=bk_t: e.tensor_copy(
                        dst[:, tt * 512:(tt + 1) * 512], bk_t[:, :]), waits=[mm], inc=dve, raw=True)
                    bk["free"] = mark(dve)
            rope_buf_reader[id(wb)] = mark(pe)
            return mark(dve)

        q1_done = rope_proj(ropeq, dWQ1)
        q2_done = rope_proj(ropet, dWQ2)
        for c in range(4):
            p.emit("vector", lambda e, c=c: e.tensor_tensor(qr12[:, c * S:(c + 1) * S], ropeq[c][:, :],
                                                            cf[:, :], ALU.mult),
                   waits=[q1_done, q2_done, consts_ld], inc=dve, raw=True)
            p.emit("vector", lambda e, c=c: e.tensor_tensor(ropeq[c][:, :], ropet[c][:, :], sf[:, :],
                                                            ALU.mult), inc=dve, raw=True)
            p.emit("vector", lambda e, c=c: e.tensor_tensor(qr12[:, c * S:(c + 1) * S],
                                                            qr12[:, c * S:(c + 1) * S], ropeq[c][:, :],
                                                            ALU.subtract), inc=dve, raw=True)
        part1 = mark(dve)
        q1s_done = rope_proj(ropeq, dWQ1s)
        q2s_done = rope_proj(ropet, dWQ2s)
        for c in range(4):
            p.emit("vector", lambda e, c=c: e.tensor_tensor(ropeq[c][:, :], ropeq[c][:, :], sf[:, :], ALU.mult),
                   waits=[q1s_done, q2s_done, part1], inc=dve, raw=True)
            p.emit("vector", lambda e, c=c: e.tensor_tensor(ropet[c][:, :], ropet[c][:, :], cf[:, :], ALU.mult),
                   inc=dve, raw=True)
            p.emit("vector", lambda e, c=c: e.tensor_tensor(ropeq[c][:, :], ropeq[c][:, :], ropet[c][:, :],
                                                            ALU.add), inc=dve, raw=True)
            p.emit("vector", lambda e, c=c: e.tensor_tensor(qr12[:, c * S:(c + 1) * S],
                                                            qr12[:, c * S:(c + 1) * S], ropeq[c][:, :],
                                                            ALU.add), inc=dve, raw=True)
        qr_done = mark(dve)

        wlv = load_w(wbufP3, dWUV.rearrange("(c p) o -> p c o", p=128), 4 * D, 4)
        p.emit("vector", lambda e: e.memset(v_a[:, :], 1.0), waits=[mark(dve), qr_done], inc=dve, raw=True)
        p.emit("vector", lambda e: e.memset(v_b[:, :], 1.0), inc=dve, raw=True)
        vset = mark(dve)
        for kt in range(NT):
            vten = v_a if kt < 8 else v_b
            ktc = kt % 8
            for ob in range(2):
                bk_t = (pC, pD)[ob]
                bk = bank[id(bk_t)]
                for k in range(4):
                    p.emit("tensor", lambda e, kt=kt, ob=ob, k=k, bk_t=bk_t: e.matmul(
                        bk_t[:, :], c_kv[:, k * S + kt * 128: k * S + (kt + 1) * 128],
                        wbufP3[:, k * D + ob * 512: k * D + (ob + 1) * 512],
                        start=(k == 0), stop=(k == 3)),
                        waits=[wlv, ckv_done, bk["free"]], inc=pe)
                mm = mark(pe)
                for hh in range(8):
                    h = ob * 8 + hh
                    p.emit("vector", lambda e, vten=vten, ktc=ktc, h=h, hh=hh, bk_t=bk_t: e.tensor_copy(
                        vten[:, ktc * 1040 + h * 65: ktc * 1040 + h * 65 + 64],
                        bk_t[:, hh * 64:(hh + 1) * 64]),
                        waits=[mm, vset], inc=dve, raw=True)
                bk["free"] = mark(dve)
        v_done = mark(dve)
        sdma("sync", mtab[:, :], d_m[:, :], s_const, waits=[mark(pe)])
        mtab_ld = mark(s_const)

    if upto >= 5:
        sc_banks2 = (scA, scB)
        LAG = 3
        tail_pend = [None]
        rcp_reader = [None]
        exps_reader = {}
        av_banks = (pE, pF)
        for ps in range(2):
            prev_attn = [mark(pe), mark(act), mark(dve)]
            proj_evac = []
            for (wd, dstT, srcT, src_ready) in ((dWUQ, q96, c_q, cq_done),
                                                (dWUK, k96, c_kv, ckv_done)):
                sdma("gpsimd", wbufQ[:, :].rearrange("p (c o) -> p c o", c=4),
                     wd.rearrange("(c p) o -> p c o", p=128), s_wq,
                     waits=prev_attn + [mark(pe)])
                wl = mark(s_wq)
                for i in range(8):
                    h = 8 * ps + i
                    for tt in range(4):
                        bk_t = (pA, pB)[(i * 4 + tt) % 2]
                        bk = bank[id(bk_t)]
                        for k in range(4):
                            p.emit("tensor", lambda e, h=h, tt=tt, k=k, bk_t=bk_t, srcT=srcT: e.matmul(
                                bk_t[0:64, :], wbufQ[:, k * D + 64 * h: k * D + 64 * h + 64],
                                srcT[:, k * S + tt * 512: k * S + (tt + 1) * 512],
                                start=(k == 0), stop=(k == 3)),
                                waits=[wl, src_ready, bk["free"]], inc=pe)
                        mm = mark(pe)
                        p.emit("vector", lambda e, i=i, tt=tt, bk_t=bk_t, dstT=dstT: e.tensor_copy(
                            dstT[0:64, i * S + tt * 512: i * S + (tt + 1) * 512], bk_t[0:64, :]),
                            waits=[mm, scat_done[0], scat_done[1]], inc=dve, raw=True)
                        bk["free"] = mark(dve)
            proj_done = mark(dve)
            for i in range(8):
                h = 8 * ps + i
                rc4, rb4 = rope_slot(h)
                sdma("sync", q96[64:96, i * S:(i + 1) * S],
                     qr12[rb4:rb4 + 32, rc4 * S:(rc4 + 1) * S], s_pk,
                     waits=[qr_done] + prev_attn + scat_done)
                sdma("sync", k96[64:96, i * S:(i + 1) * S], kr12[0:32, :], s_pk,
                     waits=[kr_done] + prev_attn + scat_done)
            pack_done = mark(s_pk)
            ready = [proj_done, pack_done, v_done, consts_ld, mtab_ld]
            for i in range(8):
                h = 8 * ps + i
                for qb in range(4):
                    nkt = 4 * qb + 4
                    avb = av_banks[(h * 4 + qb) % 2]
                    avbk = bank[id(avb)]
                    pend = []

                    def emit_av(h=h, qb=qb, nkt=nkt, avb=avb, avbk=avbk):
                        kt, sl, use_a, rdy = pend.pop(0)
                        src = exps_a if use_a else exps_b
                        p.emit("tensor", lambda e, h=h, kt=kt, sl=sl, src=src, avb=avb, nkt=nkt: e.matmul(
                            avb[0:65, :],
                            (v_a if kt < 8 else v_b)[:, (kt % 8) * 1040 + h * 65: (kt % 8) * 1040 + h * 65 + 65],
                            src[:, sl * 512:(sl + 1) * 512],
                            start=(kt == 0), stop=(kt == nkt - 1)),
                            waits=[rdy] + ([avbk["free"]] if kt == 0 else []), inc=pe)
                        exps_reader[sl] = mark(pe)

                    for kt2 in range(0, nkt, 2):
                        sb2 = sc_banks2[(kt2 // 2) % 2]
                        sbk = bank[id(sb2)]
                        sl2 = (kt2 // 2) % 2
                        for j in range(2):
                            kt = kt2 + j
                            p.emit("tensor", lambda e, i=i, kt=kt, qb=qb, sb2=sb2, j=j: e.matmul(
                                sb2[:, j * 512:(j + 1) * 512],
                                k96[0:96, i * S + kt * 128: i * S + (kt + 1) * 128],
                                q96[0:96, i * S + qb * 512: i * S + (qb + 1) * 512],
                                start=True, stop=True),
                                waits=ready + ([sbk["free"]] if j == 0 else []), inc=pe)
                        smm = mark(pe)
                        p.emit("scalar", lambda e, sl2=sl2, sb2=sb2: e.activation(
                            exps_a[:, sl2 * 1024:(sl2 + 1) * 1024], sb2[:, :], AF.Exp, scale=SCALE),
                            waits=[smm, exps_reader.get(2 * sl2), exps_reader.get(2 * sl2 + 1)],
                            inc=act, raw=True)
                        exm = mark(act)
                        sbk["free"] = exm
                        p.emit("vector", lambda e, sl2=sl2: e.tensor_scalar(
                            exps_b[:, sl2 * 1024:(sl2 + 1) * 1024], exps_a[:, sl2 * 1024:(sl2 + 1) * 1024],
                            CLO, CHI, ALU.max, ALU.min),
                            waits=[exm], inc=dve)
                        for j in range(2):
                            kt = kt2 + j
                            sl = kt % 4
                            partial = kt >= 4 * qb
                            if partial:
                                mo = qb * 512 - kt * 128 + 384
                                p.emit("vector", lambda e, sl=sl, mo=mo: e.tensor_tensor(
                                    exps_a[:, sl * 512:(sl + 1) * 512], exps_b[:, sl * 512:(sl + 1) * 512],
                                    mtab[:, mo:mo + 512], ALU.mult),
                                    waits=[mtab_ld], inc=dve, raw=True)
                            pend.append((kt, sl, partial, mark(dve)))
                        if kt2 == 2 and tail_pend[0] is not None:
                            tail_pend[0]()
                            tail_pend[0] = None
                        while len(pend) > LAG:
                            emit_av()
                    if tail_pend[0] is not None:
                        tail_pend[0]()
                        tail_pend[0] = None
                    while pend:
                        emit_av()
                    avm = mark(pe)
                    p.emit("vector", lambda e, avb=avb: e.reciprocal(rcp[:, :], avb[64:65, :]),
                           waits=[avm, rcp_reader[0]], inc=dve, raw=True)
                    rc = mark(dve)

                    def emit_tail(h=h, qb=qb, avb=avb, avbk=avbk, rc=rc):
                        ch = h // 2
                        rb = 64 * (h % 2)
                        bkD = bank[id(pG)]
                        p.emit("tensor", lambda e: e.matmul(pG[0:64, :], orowf[:, :], rcp[:, :],
                                                            start=True, stop=True),
                               waits=[rc, bkD["free"], consts_ld], inc=pe)
                        bc = mark(pe)
                        rcp_reader[0] = bc
                        p.emit("vector", lambda e: e.tensor_copy(rbsb[:, :], pG[0:64, :]),
                               waits=[bc], inc=dve, raw=True)
                        bkD["free"] = mark(dve)
                        p.emit("vector", lambda e, ch=ch, rb=rb, qb=qb, avb=avb: e.tensor_tensor(
                            attn_fm[rb:rb + 64, ch * S + qb * 512: ch * S + (qb + 1) * 512],
                            avb[0:64, :], rbsb[:, :], ALU.mult), inc=dve, raw=True)
                        avbk["free"] = mark(dve)
                    tail_pend[0] = emit_tail
        if tail_pend[0] is not None:
            tail_pend[0]()
            tail_pend[0] = None
        attn_done = mark(dve)
        attn_pe_end = mark(pe)

    if upto >= 6:
        wlo = load_w(wobuf, dWO.rearrange("(c p) o -> p c o", p=128), 8 * D, 8,
                     extra=[mark(act)])
        wo_banks = (pA, pB, pF, pG)
        st_bufs = [f32c, f32d]
        st_reader = [None, None]
        base_stores = []
        for t in range(NT):
            b = t % 2
            sdma("sync", xt[b][:, :], dx[t * 128:(t + 1) * 128, :], s_x[b], waits=[xt_reader[b]])
            xl = mark(s_x[b])
            both = []
            for dh in range(2):
                bk_t = wo_banks[(t * 2 + dh) % 4]
                bk = bank[id(bk_t)]
                for c in range(8):
                    p.emit("tensor", lambda e, t=t, dh=dh, c=c, bk_t=bk_t: e.matmul(
                        bk_t[:, :], attn_fm[:, c * S + t * 128: c * S + (t + 1) * 128],
                        wobuf[:, c * D + dh * 512: c * D + (dh + 1) * 512],
                        start=(c == 0), stop=(c == 7)),
                        waits=[wlo, attn_done, bk["free"]], inc=pe)
                mm = mark(pe)
                p.emit("vector", lambda e, b=b, dh=dh, bk_t=bk_t: e.tensor_tensor(
                    st_bufs[b][:, dh * 512:(dh + 1) * 512], bk_t[:, :],
                    xt[b][:, dh * 512:(dh + 1) * 512], ALU.add),
                    waits=[mm, xl, st_reader[b]], inc=dve, raw=True)
                bk["free"] = mark(dve)
            bs = mark(dve)
            xt_reader[b] = bs
            sdma("sync", d_base[t * 128:(t + 1) * 128, :], st_bufs[b][:, :], s_scd[b], waits=[bs])
            st_reader[b] = mark(s_scd[b])
            base_stores.append(mark(s_scd[b]))
        base_done = [mark(s_scd[0]), mark(s_scd[1])]

    ETE = mybir.EngineType
    eng_et = {"sync": ETE.SP, "tensor": ETE.PE, "scalar": ETE.Activation,
              "vector": ETE.DVE, "gpsimd": ETE.Pool}
    guard_state = {}

    def if_open(eng, fidx, gate):
        guard_state[eng] = {k: v for k, v in p.waited.items() if k[0] == eng}
        cell = {}
        guard_state[(eng, "cell")] = cell

        def th(e, fidx=fidx, gate=gate, et=eng_et[eng], cell=cell):
            e.wait_ge(gate[0].h, gate[1])
            val = nc.values_load(flgi[0:1, fidx:fidx + 1], engines=[et],
                                 min_val=0, max_val=1, skip_runtime_bounds_check=True)
            g = nc.If(val)
            g.__enter__()
            cell["g"] = g
        p.streams[eng].append(th)

    def if_close(eng, deltas=None):
        cell = guard_state[(eng, "cell")]

        def th(e, cell=cell, deltas=deltas):
            cell["g"].__exit__(None, None, None)
            if deltas:
                ge = nc.Else()
                ge.__enter__()
                for (h, d) in deltas:
                    e.memset(junk[0:1, 0:1], 0.0).then_inc(h, d)
                ge.__exit__(None, None, None)
        p.streams[eng].append(th)
        snap = guard_state[eng]
        for k in [k for k in p.waited if k[0] == eng]:
            del p.waited[k]
        p.waited.update(snap)

    if upto >= 7:
        wA_m = wB_m = b1_m = b2_m = None
        cur_par = 0

        def load_w1(ei, waits, waits_b=None):
            nonlocal wA_m, b1_m, b2_m
            sdma("gpsimd", wAa[:, :].rearrange("p (c i) -> p c i", c=8),
                 dw1[ei, :, 0:2048].rearrange("(c p) i -> p c i", p=128), s_wAa, waits=waits)
            sdma("gpsimd", wAb[:, :].rearrange("p (c i) -> p c i", c=8),
                 dw1[ei, :, 2048:4096].rearrange("(c p) i -> p c i", p=128), s_wAb,
                 waits=(waits_b if waits_b is not None else waits))
            wA_m = [mark(s_wAa), mark(s_wAb)]
            par = ei % 2
            sdma("sync", b1sb[:, par * 32:(par + 1) * 32], db1[ei, :, :], s_b1, waits=waits)
            b1_m = mark(s_b1)
            sdma("gpsimd", b2sb[:, par * D:(par + 1) * D], db2[ei:ei + 1, :], s_b2, waits=waits)
            b2_m = mark(s_b2)

        def load_w2(ei, waits, waits_a=None):
            nonlocal wB_m
            sdma("gpsimd", wBa[:, :].rearrange("p (c o) -> p c o", c=16),
                 dw2[ei, 0:2048, :].rearrange("(c p) o -> p c o", p=128), s_wBa,
                 waits=(waits_a if waits_a is not None else waits))
            sdma("gpsimd", wBb[:, :].rearrange("p (c o) -> p c o", c=16),
                 dw2[ei, 2048:4096, :].rearrange("(c p) o -> p c o", p=128), s_wBb, waits=waits)
            wB_m = [mark(s_wBa), mark(s_wBb)]

        def load_expert(ei, waits):
            load_w1(ei, waits)
            load_w2(ei, waits)

        def w1_tile(k, ic):
            ten = wAa if ic < 16 else wAb
            return ten[:, k * 2048 + (ic % 16) * 128: k * 2048 + (ic % 16) * 128 + 128]

        def w2_tile(ic2, hh):
            ten = wBa if ic2 < 16 else wBb
            return ten[:, (ic2 % 16) * D + hh * 512: (ic2 % 16) * D + (hh + 1) * 512]

        load_w1(0, [attn_pe_end] + scat_done,
                waits_b=[mark(pe)] + scat_done)
        load_w2(0, [mark(pe), mark(act), mark(dve)] + base_done)
        xg_reader = attn_pe_end
        hm_banks = (pA, pB)
        ho_banks = (pC, pD, pE, pF)
        hout_buf = [f32c, f32d]
        hout_reader = [st_reader[0], st_reader[1]]
        ffn_stores = []
        for ei in range(E):
            par = ei % 2
            soff = ei * CAP
            boff = 0
            wA_cur, b1_cur, b2_cur = wA_m, b1_m, b2_m
            wB_cur = wB_m
            w1_gate = w1_gate_a = None
            w1a_mark = w2a_mark = None
            for bi, bs_n in enumerate(BLKS):
                nct = bs_n // 128
                fidx = bi * E + ei
                for ge_ in ("sync", "tensor", "scalar", "vector"):
                    if_open(ge_, fidx, flag_gate)
                sem_snap = {id(s): s.n for s in p.all_sems}
                for c in range(nct):
                    sdma("sync", xg_st[:, c * D:(c + 1) * D],
                         d_xg[soff + boff + c * 128: soff + boff + (c + 1) * 128, :], s_xg,
                         waits=scat_done + [xg_reader])
                gl = mark(s_xg)
                bkT = bank[id(pT)]
                for c in range(nct):
                    for dc in range(8):
                        p.emit("tensor", lambda e, c=c, dc=dc: e.transpose(
                            pT[:, 0:128], xg_st[:, c * D + dc * 128: c * D + (dc + 1) * 128], ident[:, :]),
                            waits=[gl, bkT["free"]], inc=pe)
                        tm = mark(pe)
                        p.emit("scalar", lambda e, c=c, dc=dc, bs_n=bs_n: e.copy(
                            xg_fm[:, dc * bs_n + c * 128: dc * bs_n + (c + 1) * 128], pT[:, 0:128]),
                            waits=[tm], inc=act, raw=True)
                        bkT["free"] = mark(act)
                xg_done = mark(act)
                xg_reader = xg_done
                for ic in range(32):
                    bk_t = hm_banks[ic % 2]
                    bk = bank[id(bk_t)]
                    for k in range(8):
                        p.emit("tensor", lambda e, ic=ic, k=k, bk_t=bk_t, bs_n=bs_n: e.matmul(
                            bk_t[:, 0:bs_n], w1_tile(k, ic), xg_fm[:, k * bs_n:(k + 1) * bs_n],
                            start=(k == 0), stop=(k == 7)),
                            waits=[wA_cur[0 if ic < 16 else 1], xg_done, bk["free"]], inc=pe)
                    if ic == 15:
                        w1a_mark = mark(pe)
                    mm = mark(pe)
                    p.emit("scalar", lambda e, ic=ic, bk_t=bk_t, bs_n=bs_n, par=par: e.activation(
                        hmid[:, ic * 256: ic * 256 + bs_n], bk_t[:, 0:bs_n], AF.Gelu,
                        bias=b1sb[:, par * 32 + ic: par * 32 + ic + 1]),
                        waits=[mm, b1_cur], inc=act, raw=True)
                    bk["free"] = mark(act)
                hm_done = mark(act)
                if bi == len(BLKS) - 1:
                    w1_gate = [mark(pe), mark(act)]
                    w1_gate_a = [w1a_mark, mark(act)]
                for st in range(nct):
                    hv_marks = []
                    for hh in range(2):
                        bk_t = ho_banks[(st * 2 + hh) % 4]
                        bk = bank[id(bk_t)]
                        for ic2 in range(32):
                            p.emit("tensor", lambda e, st=st, hh=hh, ic2=ic2, bk_t=bk_t: e.matmul(
                                bk_t[:, :], hmid[:, ic2 * 256 + st * 128: ic2 * 256 + (st + 1) * 128],
                                w2_tile(ic2, hh), start=(ic2 == 0), stop=False),
                                waits=[wB_cur[0 if ic2 < 16 else 1], hm_done, bk["free"]], inc=pe)
                            if ic2 == 15 and st == nct - 1 and hh == 1:
                                w2a_mark = mark(pe)
                        p.emit("tensor", lambda e, hh=hh, par=par, bk_t=bk_t: e.matmul(
                            bk_t[:, :], orow[:, :], b2sb[:, par * D + hh * 512: par * D + (hh + 1) * 512],
                            start=False, stop=True),
                            waits=[b2_cur], inc=pe)
                        mm = mark(pe)
                        hb = st % 2
                        p.emit("vector", lambda e, hb=hb, hh=hh, bk_t=bk_t: e.tensor_copy(
                            hout_buf[hb][:, hh * 512:(hh + 1) * 512], bk_t[:, :]),
                            waits=[mm, hout_reader[hb]], inc=dve, raw=True)
                        bk["free"] = mark(dve)
                    hv = mark(dve)
                    hb = st % 2
                    sdma("sync", d_ffn[soff + boff + st * 128: soff + boff + (st + 1) * 128, :],
                         hout_buf[hb][:, :], s_scd[hb], waits=[hv])
                    hout_reader[hb] = mark(s_scd[hb])
                deltas = [(s.h, s.n - sem_snap[id(s)]) for s in p.all_sems
                          if s.n > sem_snap[id(s)]]
                for ge_ in ("sync", "tensor", "scalar"):
                    if_close(ge_)
                if_close("vector", deltas=deltas)
                boff += bs_n
            if ei + 1 < E:
                load_w1(ei + 1, w1_gate_a, waits_b=w1_gate)
                load_w2(ei + 1, [mark(pe)], waits_a=[w2a_mark])
        ffn_done = [mark(s_scd[0]), mark(s_scd[1])]

    if upto >= 8:
        ep_ld = [f32a, f32b]
        ep_g = [f32c, f32d]
        ld_reader = [xt_reader[0], xt_reader[1]]
        g_reader = [hout_reader[0], hout_reader[1]]
        for t in range(NT):
            b = t % 2
            sdma("sync", ep_ld[b][:, :], d_base[t * 128:(t + 1) * 128, :], s_x[b],
                 waits=[ld_reader[b], base_stores[t]])
            bl = mark(s_x[b])
            p.emit("vector", lambda e, b=b: e.memset(ep_g[b][:, :], 0.0), waits=[g_reader[b]], inc=dve, raw=True)
            ms = mark(dve)
            p.emit("gpsimd", lambda e, t=t, b=b: e.indirect_dma_start(
                ep_g[b][:, :], None, d_ffn[0:NSLOT, :], IndirectOffsetOnAxis(slot[:, t:t + 1], 0),
                bounds_check=NSLOT - 1, oob_is_err=False),
                waits=[ms] + ffn_done, inc=s_gath[b], k=16)
            gm = mark(s_gath[b])
            p.emit("vector", lambda e, t=t, b=b: e.scalar_tensor_tensor(
                ep_ld[b][:, :], ep_g[b][:, :], gate[:, t:t + 1], ep_ld[b][:, :], ALU.mult, ALU.add),
                waits=[bl, gm], inc=dve, raw=True)
            om = mark(dve)
            g_reader[b] = om
            sdma("sync", dout[t * 128:(t + 1) * 128, :], ep_ld[b][:, :], s_sab[b], waits=[om])
            ld_reader[b] = mark(s_sab[b])

    if dbg:
        reg = {
            "xn_fm": (xn_fm, 128, 8 * S, BF16),
            "gate": (gate, 128, NT, F32),
            "idxf": (idxf, 128, NT, F32),
            "logit": (logit, 128, E, F32),
            "slot": (slot, 128, NT, I32),
            "c_q": (c_q, 128, 4 * S, BF16),
            "c_kv": (c_kv, 128, 4 * S, BF16),
            "kr12": (kr12, 32, S, BF16),
            "q96": (q96, 128, 8 * S, BF16),
            "qr12": (qr12, 128, 4 * S, BF16),
            "k96": (k96, 128, 8 * S, BF16),
            "v_a": (v_a, 128, 8 * 1040, BF16),
            "v_b": (v_b, 128, 8 * 1040, BF16),
            "attn_fm": (attn_fm, 128, 8 * S, BF16),
            "base_d": (d_base, S, D, F32),
            "ffn_d": (d_ffn, NSLOT, D, F32),
            "xg_d": (d_xg, NSLOT, D, BF16),
            "xn_tok": (d_xn, S, D, BF16),
        }
        allsems = [pe, dve, act, s_const, s_wr, s_w1, s_w2, s_w3, s_wo, s_wAa, s_wAb,
                   s_wBa, s_wBb, s_b1, s_b2, s_xg] + s_scat + s_x + s_xnb + s_sxnb +             s_scd + s_sab + s_gath
        allw = [mark(s) for s in allsems]
        for name in dbg:
            ten, rows, cols, dt = reg[name]
            dd = nc.dram_tensor("dbg_" + name, [rows, cols], dt, kind="ExternalOutput").ap()
            sdma("sync", dd[:, :], ten[0:rows, 0:cols], s_sab[0], waits=allw)

    finsems = [s_sab[0], s_sab[1], s_scd[0], s_scd[1], s_gath[0], s_gath[1]] + s_scat
    p.emit("sync", lambda e: None, waits=[mark(s) for s in finsems])

    with nc.Block() as block:
        @block.sync
        def _(eng):
            for f in p.streams["sync"]:
                f(eng)

        @block.gpsimd
        def _(eng):
            for f in p.streams["gpsimd"]:
                f(eng)

        @block.tensor
        def _(eng):
            for f in p.streams["tensor"]:
                f(eng)

        @block.vector
        def _(eng):
            for f in p.streams["vector"]:
                f(eng)

        @block.scalar
        def _(eng):
            for f in p.streams["scalar"]:
                f(eng)

    return nc


def _host_weights(inp):
    o = {}
    o["wdq_t"] = np.ascontiguousarray(inp["L0_attn_W_DQ"].T)
    o["wdkv_t"] = np.ascontiguousarray(inp["L0_attn_W_DKV"].T)
    o["wuq_t"] = np.ascontiguousarray(inp["L0_attn_W_UQ"].T)
    o["wuk_t"] = np.ascontiguousarray(inp["L0_attn_W_UK"].T)
    o["wuv_t"] = np.ascontiguousarray(inp["L0_attn_W_UV"].T)
    o["wo_t"] = np.ascontiguousarray(inp["L0_attn_W_O"].T)
    wqr_t = inp["L0_attn_W_QR"].T
    q1 = np.zeros((DLAT, 512), np.float32)
    q2 = np.zeros((DLAT, 512), np.float32)
    q1s = np.zeros((DLAT, 512), np.float32)
    q2s = np.zeros((DLAT, 512), np.float32)
    for h in range(H):
        c4, rb4 = rope_slot(h)
        co = c4 * 128 + rb4
        q1[:, co:co + 16] = wqr_t[:, 32 * h + 0:32 * h + 32:2]
        q2[:, co:co + 16] = wqr_t[:, 32 * h + 1:32 * h + 32:2]
        q1s[:, co + 16:co + 32] = wqr_t[:, 32 * h + 0:32 * h + 32:2]
        q2s[:, co + 16:co + 32] = wqr_t[:, 32 * h + 1:32 * h + 32:2]
    o["wqr1"], o["wqr2"], o["wqr1s"], o["wqr2s"] = q1, q2, q1s, q2s
    wkr_t = inp["L0_attn_W_KR"].T
    kp = np.zeros((D, 64), np.float32)
    kps = np.zeros((D, 64), np.float32)
    kp[:, 0:16] = wkr_t[:, 0::2]
    kp[:, 32:48] = wkr_t[:, 1::2]
    kps[:, 16:32] = wkr_t[:, 1::2]
    kps[:, 48:64] = wkr_t[:, 0::2]
    o["wkrp"], o["wkrps"] = kp, kps
    o["wr"] = np.ascontiguousarray(inp["L0_ffn_wr"])
    o["br"] = np.ascontiguousarray(inp["L0_ffn_br"].reshape(1, E))
    o["w1"] = np.ascontiguousarray(inp["L0_ffn_w1"])
    o["b1t"] = np.ascontiguousarray(inp["L0_ffn_b1"].reshape(E, 32, 128).transpose(0, 2, 1))
    o["w2"] = np.ascontiguousarray(inp["L0_ffn_w2"])
    o["b2"] = np.ascontiguousarray(inp["L0_ffn_b2"])
    return o


def kernel(**inputs):
    x = inputs["L0_attn_x"]
    w = _host_weights(inputs)
    nc = build_core()
    in_maps = []
    for b in range(8):
        m = {"x": np.ascontiguousarray(x[b])}
        m.update(w)
        in_maps.append(m)
    res = run_bass_kernel_spmd(nc, in_maps, core_ids=list(range(8)))
    return np.stack([r["out"] for r in res.results], axis=0)


if __name__ == "__main__":
    nc = build_core()
    print("built ok; instructions:", nc.n_instructions())
    print("sbuf remaining:", nc.sbuf_bytes_remaining)


# version 18 (final)
# speedup vs baseline: 1.1327x; 1.0043x over previous
import math
import numpy as np
import ml_dtypes
import concourse.bass as bass
import concourse.mybir as mybir
from concourse.bass import IndirectOffsetOnAxis
from concourse.bass_utils import run_bass_kernel_spmd

F32 = mybir.dt.float32
BF16 = mybir.dt.bfloat16
I32 = mybir.dt.int32
AF = mybir.ActivationFunctionType
ALU = mybir.AluOpType
AX = mybir.AxisListType.X

S = 2048
D = 1024
H = 16
DHR = 32
DLAT = 512
E = 8
DI = 4096
CAP = 1152
NSLOT = E * CAP
NT = S // 128
SCALE = 1.0 / math.sqrt(96.0)
CLO = math.exp(-80.0 * SCALE)
CHI = math.exp(80.0 * SCALE)
BLKS = [256, 256, 256, 256, 128]

ENGINES = ("sync", "gpsimd", "tensor", "vector", "scalar")


def rope_slot(h):
    return h // 4, 32 * (h % 4)


class SemC:
    __slots__ = ("h", "n", "name", "is_dma")

    def __init__(self, h, name):
        self.h = h
        self.n = 0
        self.name = name
        self.is_dma = False


class Prog:
    def __init__(self, nc):
        self.nc = nc
        self.streams = {e: [] for e in ENGINES}
        self.waited = {}
        self.eng_sems = {}
        self.all_sems = []

    def sem(self, name):
        s = SemC(self.nc.alloc_semaphore(name), name)
        self.all_sems.append(s)
        return s

    def emit(self, eng, build, waits=(), inc=None, k=1, raw=False):
        if raw and eng in self.eng_sems:
            s = self.eng_sems[eng]
            waits = list(waits) + [(s, s.n)]
        wl = []
        for w in waits:
            if w is None:
                continue
            s, v = (w if isinstance(w, tuple) else (w, w.n))
            if v <= 0:
                continue
            key = (eng, id(s))
            if self.waited.get(key, -1) < v:
                self.waited[key] = v
                wl.append((s.h, v))
        if inc is not None:
            inc.n += k
        hk = inc.h if inc is not None else None

        def th(e, build=build, wl=wl, hk=hk, k=k):
            for (h, v) in wl:
                e.wait_ge(h, v)
            ins = build(e)
            if hk is not None and ins is not None:
                ins.then_inc(hk, k)
        self.streams[eng].append(th)


def _cosf_sinf():
    inv = 10000.0 ** (-np.arange(0, DHR, 2, dtype=np.float64) / DHR)
    pos = np.arange(S, dtype=np.float64)
    ang = pos[None, :] * inv[:, None]
    cf = np.tile(np.cos(ang), (8, 1)).astype(ml_dtypes.bfloat16)
    sf = np.tile(np.sin(ang), (8, 1)).astype(ml_dtypes.bfloat16)
    return cf, sf


def _mtab():
    kp = np.arange(128)[:, None]
    c = np.arange(896)[None, :]
    return (kp <= c - 384).astype(ml_dtypes.bfloat16)


def build_core(upto=99, dbg=None):
    nc = bass.Bass("TRN2", target_bir_lowering=False)
    p = Prog(nc)

    dx = nc.dram_tensor("x", [S, D], F32, kind="ExternalInput").ap()
    dWDQ = nc.dram_tensor("wdq_t", [D, DLAT], F32, kind="ExternalInput").ap()
    dWDKV = nc.dram_tensor("wdkv_t", [D, DLAT], F32, kind="ExternalInput").ap()
    dWUQ = nc.dram_tensor("wuq_t", [DLAT, D], F32, kind="ExternalInput").ap()
    dWUK = nc.dram_tensor("wuk_t", [DLAT, D], F32, kind="ExternalInput").ap()
    dWUV = nc.dram_tensor("wuv_t", [DLAT, D], F32, kind="ExternalInput").ap()
    dWQ1 = nc.dram_tensor("wqr1", [DLAT, 512], F32, kind="ExternalInput").ap()
    dWQ2 = nc.dram_tensor("wqr2", [DLAT, 512], F32, kind="ExternalInput").ap()
    dWQ1s = nc.dram_tensor("wqr1s", [DLAT, 512], F32, kind="ExternalInput").ap()
    dWQ2s = nc.dram_tensor("wqr2s", [DLAT, 512], F32, kind="ExternalInput").ap()
    dWKR = nc.dram_tensor("wkrp", [D, 64], F32, kind="ExternalInput").ap()
    dWKRs = nc.dram_tensor("wkrps", [D, 64], F32, kind="ExternalInput").ap()
    dWO = nc.dram_tensor("wo_t", [D, D], F32, kind="ExternalInput").ap()
    dwr = nc.dram_tensor("wr", [D, E], F32, kind="ExternalInput").ap()
    dbr = nc.dram_tensor("br", [1, E], F32, kind="ExternalInput").ap()
    dw1 = nc.dram_tensor("w1", [E, D, DI], F32, kind="ExternalInput").ap()
    db1 = nc.dram_tensor("b1t", [E, 128, 32], F32, kind="ExternalInput").ap()
    dw2 = nc.dram_tensor("w2", [E, DI, D], F32, kind="ExternalInput").ap()
    db2 = nc.dram_tensor("b2", [E, D], F32, kind="ExternalInput").ap()
    dout = nc.dram_tensor("out", [S, D], F32, kind="ExternalOutput").ap()

    d_xn = nc.dram_tensor("xn_tok", [S, D], BF16, kind="Internal").ap()
    d_xg = nc.dram_tensor("xg", [NSLOT, D], BF16, kind="Internal").ap()
    d_ffn = nc.dram_tensor("ffn", [NSLOT, D], F32, kind="Internal").ap()
    d_base = nc.dram_tensor("base", [S, D], F32, kind="Internal").ap()

    cf_np, sf_np = _cosf_sinf()
    d_cf = nc.inline_tensor(np.ascontiguousarray(cf_np), "c_cf").ap()
    d_sf = nc.inline_tensor(np.ascontiguousarray(sf_np), "c_sf").ap()
    d_m = nc.inline_tensor(np.ascontiguousarray(_mtab()), "c_mtab").ap()
    d_id = nc.inline_tensor(np.eye(128, dtype=ml_dtypes.bfloat16), "c_ident").ap()
    ut = (np.arange(128)[:, None] < np.arange(128)[None, :]).astype(ml_dtypes.bfloat16)
    d_ut = nc.inline_tensor(ut, "c_utri").ap()
    d_on = nc.inline_tensor(np.ones((128, 128), dtype=ml_dtypes.bfloat16), "c_ones").ap()
    d_or = nc.inline_tensor(np.ones((1, 128), dtype=ml_dtypes.bfloat16), "c_onesrow").ap()
    d_orf = nc.inline_tensor(np.ones((1, 64), dtype=np.float32), "c_onesrowf").ap()
    iota_np = np.broadcast_to(np.arange(E, dtype=np.float32), (128, E)).copy()
    d_io = nc.inline_tensor(iota_np, "c_iota8").ap()

    TOP = nc.sbuf_top
    KB = 1024
    MAPKB = 207.0
    BASE0 = TOP - int(MAPKB * KB)
    assert BASE0 >= nc.sbuf_base, (BASE0, nc.sbuf_base)
    _placed = []

    def at(name, shape, dtype, off_kb):
        esz = {F32: 4, BF16: 2, I32: 4}[dtype]
        nbytes = int(np.prod(shape[1:])) * esz
        off = BASE0 + int(off_kb * KB)
        assert off + nbytes <= TOP, (name, off_kb, nbytes)
        _placed.append((name, off, off + nbytes))
        return nc.alloc_sbuf_tensor_at(name, list(shape), dtype, offset=off)

    f32a = at("f32a", [128, D], F32, 0)
    f32b = at("f32b", [128, D], F32, 4)
    ident = at("ident", [128, 128], BF16, 8)
    utri = at("utri", [128, 128], BF16, 8.25)
    ones = at("ones", [128, 128], BF16, 8.5)
    orow = at("orow", [1, 128], BF16, 8.75)
    orowf = at("orowf", [1, 64], F32, 9.0)
    iot = at("iota8", [128, E], F32, 9.25)
    logit = at("logit", [128, E], F32, 9.5)
    s8a = at("s8a", [128, E], F32, 9.5625)
    s8b = at("s8b", [128, E], F32, 9.625)
    m1t = at("m1t", [128, 16], F32, 9.6875)
    oh_f = at("oh_f", [128, NT * E], F32, 9.75)
    oh_b = at("oh_b", [128, NT * E], BF16, 10.25)
    gate = at("gate", [128, NT], F32, 10.5)
    idxf = at("idxf", [128, NT], F32, 10.5625)
    slot = at("slot", [128, NT], I32, 10.625)
    rankf = at("rankf", [128, E], F32, 10.6875)
    sltf = at("sltf", [128, 4], F32, 10.75)
    wrsb = at("wrsb", [128, E * 8], BF16, 10.8125)
    brsb = at("brsb", [1, E], BF16, 10.9375)
    cntf = at("cntf", [1, E], F32, 11.0)
    flgf = at("flgf", [1, 64], F32, 11.0625)
    flgi = at("flgi", [1, 64], I32, 11.3125)
    junk = at("junk", [1, 4], F32, 11.5625)
    kr12 = at("kr12", [32, S], BF16, 11.625)
    EB = 16.0
    xn_fm = at("xn_fm", [128, 8 * S], BF16, EB)
    ropeq = [at(f"ropeq{i}", [128, S], BF16, EB + 4 * i) for i in range(4)]
    ropet = [at(f"ropet{i}", [128, S], BF16, EB + 16 + 4 * i) for i in range(4)]
    attn_fm = at("attn_fm", [128, 8 * S], BF16, EB)
    c_q = at("c_q", [128, 4 * S], BF16, EB + 32)
    c_kv = at("c_kv", [128, 4 * S], BF16, EB + 48)
    q96 = at("q96", [128, 8 * S], BF16, EB + 64)
    wbufP1 = at("wbufP1", [128, 4096], BF16, EB + 64)
    kpre = at("kpre", [64, S], BF16, EB + 72)
    kpres = at("kpres", [64, S], BF16, EB + 76)
    ktmp = at("ktmp", [32, S], BF16, EB + 80)
    ktmp2 = at("ktmp2", [32, S], BF16, EB + 84)
    xnb0 = at("xnb0", [128, D], BF16, EB + 88)
    xnb1 = at("xnb1", [128, D], BF16, EB + 90)
    k96 = at("k96", [128, 8 * S], BF16, EB + 96)
    wbufP2 = at("wbufP2", [128, 4096], BF16, EB + 96)
    wbufP3 = at("wbufP3", [128, 4096], BF16, EB + 104)
    wobuf = at("wobuf", [128, 8 * D], BF16, EB + 96)
    cf = at("cf", [128, S], BF16, EB + 128)
    sf = at("sf", [128, S], BF16, EB + 132)
    v_a = at("v_a", [128, 8 * 1040], BF16, EB + 128)
    v_b = at("v_b", [128, 8 * 1040], BF16, EB + 144.25)
    rnka = at("rnka", [128, NT * E], F32, EB + 144.25)
    rsel = at("rsel", [128, NT], F32, EB + 144.75)
    ovr16 = at("ovr16", [128, NT], F32, EB + 144.8125)
    slf16 = at("slf16", [128, NT], F32, EB + 144.875)
    qr12 = at("qr12", [128, 4 * S], BF16, EB + 160.5)
    mtab = at("mtab", [128, 896], BF16, EB + 176.5)
    rcp = at("rcp", [1, 512], F32, EB + 178.5)
    rbsb = at("rbsb", [64, 512], F32, EB + 180.5)
    exps_a = at("exps_a", [128, 4 * 512], BF16, EB + 182.5)
    exps_b = at("exps_b", [128, 4 * 512], BF16, EB + 186.5)
    wbufQ = at("wbufQ", [128, 4096], BF16, EB + 182.5)
    wAa = at("wAa", [128, 16384], BF16, EB + 64)
    wAb = at("wAb", [128, 16384], BF16, EB + 96)
    wBa = at("wBa", [128, 16384], BF16, EB)
    wBb = at("wBb", [128, 16384], BF16, EB + 32)
    hmid = at("hmid", [128, 32 * 256], BF16, EB + 128)
    xg_st = at("xg_st", [128, 2 * D], BF16, EB + 144)
    xg_fm = at("xg_fm", [128, 8 * 256], BF16, EB + 148)
    b1sb = at("b1sb", [128, 64], F32, EB + 152)
    b2sb = at("b2sb", [1, 2 * D], BF16, EB + 152.25)
    f32c = at("f32c", [128, D], F32, EB + 176.5)
    f32d = at("f32d", [128, D], F32, EB + 180.5)

    PS4 = nc.alloc_psum_tensor("PS4", [128, 2048], F32)
    pA = PS4[:, 0:512]
    pB = PS4[:, 512:1024]
    pC = PS4[:, 1024:1536]
    pD = PS4[:, 1536:2048]
    pE = nc.alloc_psum_tensor("pE", [128, 512], F32)
    pF = nc.alloc_psum_tensor("pF", [128, 512], F32)
    pG = nc.alloc_psum_tensor("pG", [128, 512], F32)
    pT = nc.alloc_psum_tensor("pT", [128, 1024], BF16)

    s_const = p.sem("s_const")
    s_wr = p.sem("s_wr")
    s_x = [p.sem("s_x0"), p.sem("s_x1")]
    s_xnb = [p.sem("s_xnb0"), p.sem("s_xnb1")]
    s_sxnb = [p.sem("s_sxnb0"), p.sem("s_sxnb1")]
    s_scd = [p.sem("s_scd0"), p.sem("s_scd1")]
    s_sab = [p.sem("s_sab0"), p.sem("s_sab1")]
    s_w1 = p.sem("s_w1")
    s_w2 = p.sem("s_w2")
    s_w3 = p.sem("s_w3")
    s_wo = p.sem("s_wo")
    s_wAa = p.sem("s_wAa")
    s_wAb = p.sem("s_wAb")
    s_wBa = p.sem("s_wBa")
    s_wBb = p.sem("s_wBb")
    s_b1 = p.sem("s_b1")
    s_b2 = p.sem("s_b2")
    s_scat = [p.sem("s_scat0"), p.sem("s_scat1")]
    s_gath = [p.sem("s_gath0"), p.sem("s_gath1")]
    s_zero = p.sem("s_zero")
    s_wq = p.sem("s_wq")
    s_pk = p.sem("s_pk")
    s_xg = p.sem("s_xg")
    wbuf_sem = {}
    pe = p.sem("pe")
    act = p.sem("act")
    dve = p.sem("dve")
    p.eng_sems = {"vector": dve, "scalar": act}

    def sdma(eng, out_ap, in_ap, sem, waits=()):
        p.emit(eng, lambda e, o=out_ap, i=in_ap: e.dma_start(out=o, in_=i),
               waits=waits, inc=sem, k=16)

    def mark(sem):
        return (sem, sem.n)

    wbuf_sem.update({id(wbufP1): s_w1, id(wbufP2): s_w2, id(wbufP3): s_w3, id(wobuf): s_wo})

    scA = PS4[:, 0:1024]
    scB = PS4[:, 1024:2048]
    bank = {id(b): {"free": None} for b in (pA, pB, pC, pD, pE, pF, pG, pT, scA, scB)}

    def load_w(buf, dram_ap, ncols, nchunk, extra=(), conservative=True):
        sem = wbuf_sem[id(buf)]
        w = ([mark(pe), mark(dve)] if conservative else []) + list(extra)
        sdma("gpsimd", buf[:, 0:ncols].rearrange("p (c o) -> p c o", c=nchunk), dram_ap,
             sem, waits=w)
        return mark(sem)

    sdma("sync", cf[:, :], d_cf[:, :], s_const)
    sdma("sync", sf[:, :], d_sf[:, :], s_const)
    sdma("sync", ident[:, :], d_id[:, :], s_const)
    sdma("sync", utri[:, :], d_ut[:, :], s_const)
    sdma("sync", ones[:, :], d_on[:, :], s_const)
    sdma("sync", orow[:, :], d_or[:, :], s_const)
    sdma("sync", orowf[:, :], d_orf[:, :], s_const)
    sdma("sync", iot[:, :], d_io[:, :], s_const)
    sdma("gpsimd", wrsb[:, :].rearrange("p (c e) -> p c e", c=8), dwr.rearrange("(c p) e -> p c e", p=128), s_wr)
    sdma("gpsimd", brsb[:, :], dbr[:, :], s_wr)
    consts_ld = mark(s_const)
    wr_ld = mark(s_wr)
    pool = p.sem("pool")
    p.emit("gpsimd", lambda e: e.memset(xnb0[:, :], 0.0), inc=pool)
    zmemset = mark(pool)
    for zi in range(NSLOT // 128):
        sdma("gpsimd", d_xg[zi * 128:(zi + 1) * 128, :], xnb0[:, :], s_zero, waits=[zmemset])
    zero_done = mark(s_zero)

    xt = [f32a, f32b]
    xnb = [xnb0, xnb1]
    xt_reader = [None, None]
    xnb_reader = [[zero_done], None]
    xn_store = []
    oh_done = []
    pa_free = None

    if upto >= 1:
        for t in range(NT):
            b = t % 2
            sdma("sync", xt[b][:, :], dx[t * 128:(t + 1) * 128, :], s_x[b],
                 waits=[xt_reader[b]])
            ldm = mark(s_x[b])
            Q = (t % 2) * 8
            p.emit("vector", lambda e, b=b, Q=Q: e.reduce_sum(m1t[:, Q:Q + 1], xt[b][:, :], axis=AX),
                   waits=[ldm], inc=dve, raw=True)
            p.emit("vector", lambda e, Q=Q: e.tensor_scalar_mul(m1t[:, Q + 1:Q + 2], m1t[:, Q:Q + 1],
                                                                1.0 / D), inc=dve, raw=True)
            p.emit("vector", lambda e, b=b, Q=Q: e.tensor_scalar(xt[b][:, :], xt[b][:, :],
                                                                 m1t[:, Q + 1:Q + 2], None,
                                                                 ALU.subtract), inc=dve, raw=True)
            cent = mark(dve)
            p.emit("scalar", lambda e, b=b, Q=Q: e.activation(xnb[b][:, :], xt[b][:, :], AF.Square,
                                                              accum_out=m1t[:, Q + 2:Q + 3]),
                   waits=[cent] + (xnb_reader[b] or []), inc=act, raw=True)
            sq = mark(act)
            p.emit("vector", lambda e, Q=Q: e.tensor_scalar(m1t[:, Q + 3:Q + 4], m1t[:, Q + 2:Q + 3],
                                                            1.0 / D, 1e-5,
                                                            ALU.mult, ALU.add), waits=[sq], inc=dve, raw=True)
            vpe = mark(dve)
            p.emit("scalar", lambda e, Q=Q: e.activation(m1t[:, Q + 1:Q + 2], m1t[:, Q + 3:Q + 4], AF.Ln),
                   waits=[vpe], inc=act, raw=True)
            p.emit("scalar", lambda e, Q=Q: e.activation(m1t[:, Q + 2:Q + 3], m1t[:, Q + 1:Q + 2],
                                                         AF.Exp, scale=-0.5), inc=act, raw=True)
            rstd = mark(act)
            p.emit("vector", lambda e, b=b, Q=Q: e.tensor_scalar(xnb[b][:, :], xt[b][:, :],
                                                                 m1t[:, Q + 2:Q + 3], None,
                                                                 ALU.mult), waits=[rstd], inc=dve, raw=True)
            xnm = mark(dve)
            xt_reader[b] = xnm
            sdma("sync", d_xn[t * 128:(t + 1) * 128, :], xnb[b][:, :], s_sxnb[b], waits=[xnm])
            xn_store.append(mark(s_sxnb[b]))
            for c in range(8):
                tb = pT[:, 0:128] if c % 2 == 0 else pC[:, 0:64].bitcast(BF16)
                tkey = id(pT) if c % 2 == 0 else id(pC)
                p.emit("tensor", lambda e, b=b, c=c, tb=tb: e.transpose(tb, xnb[b][:, c * 128:(c + 1) * 128],
                                                                        ident[:, :]),
                       waits=[xnm, consts_ld, bank[tkey]["free"]], inc=pe)
                tm = mark(pe)
                if c % 2 == 0:
                    p.emit("scalar", lambda e, c=c, t=t, tb=tb: e.copy(
                        xn_fm[:, c * S + t * 128: c * S + (t + 1) * 128], tb),
                        waits=[tm], inc=act, raw=True)
                    bank[tkey]["free"] = mark(act)
                else:
                    p.emit("vector", lambda e, c=c, t=t, tb=tb: e.tensor_copy(
                        xn_fm[:, c * S + t * 128: c * S + (t + 1) * 128], tb),
                        waits=[tm], inc=dve, raw=True)
                    bank[tkey]["free"] = mark(dve)
            xnb_reader[b] = [mark(act), mark(dve), mark(s_sxnb[b])]
            xnfm_t = [mark(act), mark(dve)]
            for c in range(8):
                p.emit("tensor", lambda e, c=c, t=t: e.matmul(pA[:, 0:E],
                                                              xn_fm[:, c * S + t * 128: c * S + (t + 1) * 128],
                                                              wrsb[:, c * E:(c + 1) * E],
                                                              start=(c == 0), stop=False),
                       waits=xnfm_t + [wr_ld, pa_free], inc=pe)
            p.emit("tensor", lambda e: e.matmul(pA[:, 0:E], orow[:, :], brsb[:, :], start=False, stop=True),
                   waits=[consts_ld], inc=pe)
            rmm = mark(pe)
            p.emit("vector", lambda e: e.tensor_copy(logit[:, :], pA[:, 0:E]), waits=[rmm], inc=dve, raw=True)
            pa_free = mark(dve)
            p.emit("vector", lambda e, Q=Q: e.reduce_max(m1t[:, Q + 4:Q + 5], logit[:, :], axis=AX), inc=dve, raw=True)
            p.emit("vector", lambda e, t=t, Q=Q: e.tensor_scalar(oh_f[:, t * E:(t + 1) * E], logit[:, :],
                                                                 m1t[:, Q + 4:Q + 5], 0.0,
                                                                 ALU.subtract, ALU.is_ge), inc=dve, raw=True)
            p.emit("vector", lambda e, t=t: e.tensor_copy(oh_b[:, t * E:(t + 1) * E], oh_f[:, t * E:(t + 1) * E]),
                   inc=dve, raw=True)
            p.emit("vector", lambda e, t=t: e.tensor_tensor(s8a[:, :], oh_f[:, t * E:(t + 1) * E], iot[:, :],
                                                            ALU.mult), waits=[consts_ld], inc=dve, raw=True)
            p.emit("vector", lambda e, t=t: e.reduce_sum(idxf[:, t:t + 1], s8a[:, :], axis=AX), inc=dve, raw=True)
            p.emit("vector", lambda e, t=t: e.scalar_tensor_tensor(s8b[:, :], oh_f[:, t * E:(t + 1) * E], -1e30,
                                                                   logit[:, :], ALU.mult, ALU.add), inc=dve, raw=True)
            p.emit("vector", lambda e, Q=Q: e.reduce_max(m1t[:, Q + 5:Q + 6], s8b[:, :], axis=AX), inc=dve, raw=True)
            p.emit("vector", lambda e, Q=Q: e.tensor_tensor(m1t[:, Q + 6:Q + 7], m1t[:, Q + 4:Q + 5],
                                                            m1t[:, Q + 5:Q + 6], ALU.subtract),
                   inc=dve, raw=True)
            dl = mark(dve)
            p.emit("scalar", lambda e, Q=Q: e.activation(m1t[:, Q + 7:Q + 8], m1t[:, Q + 6:Q + 7],
                                                         AF.Exp, scale=-1.0),
                   waits=[dl], inc=act, raw=True)
            ge = mark(act)
            p.emit("vector", lambda e, Q=Q: e.tensor_scalar(m1t[:, Q + 6:Q + 7], m1t[:, Q + 7:Q + 8],
                                                            1.0, None, ALU.add),
                   waits=[ge], inc=dve, raw=True)
            p.emit("vector", lambda e, t=t, Q=Q: e.reciprocal(gate[:, t:t + 1], m1t[:, Q + 6:Q + 7]), inc=dve, raw=True)
            oh_done.append(mark(dve))

    scat_done = None
    if upto >= 2:
        allo = oh_done[-1]
        pb_free = None
        for t in range(NT):
            p.emit("tensor", lambda e, t=t: e.matmul(pB[:, 0:E], utri[:, :], oh_b[:, t * E:(t + 1) * E],
                                                     start=True, stop=(t == 0)),
                   waits=[allo, consts_ld, pb_free], inc=pe)
            for tp in range(t):
                p.emit("tensor", lambda e, tp=tp, t=t: e.matmul(pB[:, 0:E], ones[:, :],
                                                                oh_b[:, tp * E:(tp + 1) * E],
                                                                start=False, stop=(tp == t - 1)), inc=pe)
            rk = mark(pe)
            p.emit("vector", lambda e, t=t: e.tensor_copy(rnka[:, t * E:(t + 1) * E], pB[:, 0:E]),
                   waits=[rk], inc=dve, raw=True)
            pb_free = mark(dve)
        p.emit("vector", lambda e: e.tensor_tensor(oh_f[:, :], rnka[:, :], oh_f[:, :], ALU.mult),
               inc=dve, raw=True)
        p.emit("vector", lambda e: e.reduce_sum(rsel[:, :],
                                                oh_f[:, :].rearrange("p (t e) -> p t e", e=E),
                                                axis=AX), inc=dve, raw=True)
        p.emit("vector", lambda e: e.scalar_tensor_tensor(slf16[:, :], idxf[:, :], float(CAP),
                                                          rsel[:, :], ALU.mult, ALU.add), inc=dve, raw=True)
        p.emit("vector", lambda e: e.tensor_scalar(ovr16[:, :], rsel[:, :], float(CAP) - 0.5, 0.0,
                                                   ALU.subtract, ALU.is_ge), inc=dve, raw=True)
        p.emit("vector", lambda e: e.scalar_tensor_tensor(slf16[:, :], ovr16[:, :], 1e7,
                                                          slf16[:, :], ALU.mult, ALU.add), inc=dve, raw=True)
        p.emit("vector", lambda e: e.tensor_copy(slot[:, :], slf16[:, :]), inc=dve, raw=True)
        sl = mark(dve)
        for t in range(NT):
            b = t % 2
            sdma("sync", xnb[b][:, :], d_xn[t * 128:(t + 1) * 128, :], s_xnb[b],
                 waits=[xn_store[t]] + (xnb_reader[b] or []))
            xr = mark(s_xnb[b])
            p.emit("gpsimd", lambda e, t=t, b=b: e.indirect_dma_start(
                d_xg[0:NSLOT, :], IndirectOffsetOnAxis(slot[:, t:t + 1], 0),
                xnb[b][:, :], None,
                bounds_check=NSLOT - 1, oob_is_err=False),
                waits=[sl, xr, zero_done], inc=s_scat[b], k=16)
            xnb_reader[b] = [mark(s_scat[b])]
        scat_done = [mark(s_scat[0]), mark(s_scat[1])]
        for t in range(NT):
            p.emit("tensor", lambda e, t=t: e.matmul(pC[0:1, 0:E], ones[:, 0:1],
                                                     oh_b[:, t * E:(t + 1) * E],
                                                     start=(t == 0), stop=(t == NT - 1)),
                   waits=[allo, consts_ld], inc=pe)
        cmm = mark(pe)
        p.emit("vector", lambda e: e.tensor_copy(cntf[:, :], pC[0:1, 0:E]), waits=[cmm], inc=dve, raw=True)
        bank[id(pC)]["free"] = mark(dve)
        boff_acc = 0
        for b5 in range(len(BLKS)):
            p.emit("vector", lambda e, b5=b5, bo=boff_acc: e.tensor_scalar(
                flgf[0:1, b5 * E:(b5 + 1) * E], cntf[0:1, :], bo + 0.5, 0.0,
                ALU.subtract, ALU.is_ge), inc=dve, raw=True)
            boff_acc += BLKS[b5]
        p.emit("vector", lambda e: e.tensor_copy(flgi[0:1, 0:len(BLKS) * E], flgf[0:1, 0:len(BLKS) * E]),
               inc=dve, raw=True)
        flag_gate = mark(dve)

    def proj_fm(dst, w_dram, wbuf, kdim, odim, rhs, rhs_ready, psums, dve_out=True, first=False):
        kc = kdim // 128
        oc = odim // 128
        wl = load_w(wbuf, w_dram.rearrange("(c p) o -> p c o", p=128), kc * odim, kc,
                    conservative=not first)
        done = None
        for o in range(oc):
            for tt in range(4):
                bk_t = psums[(o * 4 + tt) % len(psums)]
                bk = bank[id(bk_t)]
                for k in range(kc):
                    p.emit("tensor", lambda e, o=o, tt=tt, k=k, bk_t=bk_t, wbuf=wbuf, odim=odim, kc=kc, rhs=rhs:
                           e.matmul(bk_t[:, :],
                                    wbuf[:, k * odim + o * 128: k * odim + (o + 1) * 128],
                                    rhs[:, k * S + tt * 512: k * S + (tt + 1) * 512],
                                    start=(k == 0), stop=(k == kc - 1)),
                           waits=[wl, bk["free"]] + (rhs_ready if isinstance(rhs_ready, list) else [rhs_ready]), inc=pe)
                mm = mark(pe)
                p.emit("vector", lambda e, o=o, tt=tt, bk_t=bk_t, dst=dst: e.tensor_copy(
                    dst[:, o * S + tt * 512: o * S + (tt + 1) * 512], bk_t[:, :]),
                    waits=[mm], inc=dve, raw=True)
                bk["free"] = mark(dve)
        return mark(dve)

    if upto >= 3:
        xn_last = mark(act)
        xn_last_d = mark(dve)
        cq_done = proj_fm(c_q, dWDQ, wbufP1, D, DLAT, xn_fm, [xn_last, xn_last_d], (pA, pB),
                          first=True)
        ckv_done = proj_fm(c_kv, dWDKV, wbufP1, D, DLAT, xn_fm, [xn_last, xn_last_d], (pA, pB))
        for (wd, dst) in ((dWKR, kpre), (dWKRs, kpres)):
            wl = load_w(wbufP1, wd.rearrange("(c p) o -> p c o", p=128), 8 * 64, 8)
            for tt in range(4):
                bk = bank[id(pC)]
                for k in range(8):
                    p.emit("tensor", lambda e, tt=tt, k=k: e.matmul(
                        pC[0:64, :], wbufP1[:, k * 64:(k + 1) * 64],
                        xn_fm[:, k * S + tt * 512: k * S + (tt + 1) * 512],
                        start=(k == 0), stop=(k == 7)),
                        waits=[wl, xn_last, xn_last_d, bk["free"]], inc=pe)
                mm = mark(pe)
                p.emit("vector", lambda e, dst=dst, tt=tt: e.tensor_copy(dst[:, tt * 512:(tt + 1) * 512],
                                                                         pC[0:64, :]),
                       waits=[mm], inc=dve, raw=True)
                bk["free"] = mark(dve)
        kp_done = mark(dve)
        p.emit("vector", lambda e: e.tensor_tensor(ktmp[:, :], kpre[0:32, :], cf[0:32, :], ALU.mult),
               waits=[kp_done, consts_ld], inc=dve, raw=True)
        p.emit("vector", lambda e: e.tensor_tensor(ktmp2[:, :], kpre[32:64, :], sf[32:64, :], ALU.mult), inc=dve, raw=True)
        p.emit("vector", lambda e: e.tensor_tensor(ktmp[:, :], ktmp[:, :], ktmp2[:, :], ALU.subtract), inc=dve, raw=True)
        p.emit("vector", lambda e: e.tensor_tensor(ktmp2[:, :], kpres[32:64, :], sf[32:64, :], ALU.mult), inc=dve, raw=True)
        p.emit("vector", lambda e: e.tensor_tensor(kr12[0:32, :], kpres[0:32, :], cf[0:32, :], ALU.mult), inc=dve, raw=True)
        p.emit("vector", lambda e: e.tensor_tensor(ktmp2[:, :], ktmp2[:, :], kr12[0:32, :], ALU.add), inc=dve, raw=True)
        p.emit("vector", lambda e: e.tensor_tensor(kr12[0:32, :], ktmp[:, :], ktmp2[:, :], ALU.add), inc=dve, raw=True)
        kr_done = mark(dve)

    if upto >= 4:
        rope_cnt = [0]
        rope_buf_reader = {id(wbufP2): None, id(wbufP3): None}

        def rope_proj(dst_list, w_dram):
            wb = (wbufP2, wbufP3)[rope_cnt[0] % 2]
            rope_cnt[0] += 1
            wl = load_w(wb, w_dram.rearrange("(c p) o -> p c o", p=128), 4 * 512, 4,
                        conservative=False, extra=[rope_buf_reader[id(wb)]])
            for c in range(4):
                for tt in range(4):
                    bk_t = (pA, pB)[tt % 2]
                    bk = bank[id(bk_t)]
                    for k in range(4):
                        p.emit("tensor", lambda e, c=c, tt=tt, k=k, bk_t=bk_t, wb=wb: e.matmul(
                            bk_t[:, :], wb[:, k * 512 + c * 128: k * 512 + (c + 1) * 128],
                            c_q[:, k * S + tt * 512: k * S + (tt + 1) * 512],
                            start=(k == 0), stop=(k == 3)),
                            waits=[wl, cq_done, bk["free"]], inc=pe)
                    mm = mark(pe)
                    p.emit("vector", lambda e, dst=dst_list[c], tt=tt, bk_t=bk_t: e.tensor_copy(
                        dst[:, tt * 512:(tt + 1) * 512], bk_t[:, :]), waits=[mm], inc=dve, raw=True)
                    bk["free"] = mark(dve)
            rope_buf_reader[id(wb)] = mark(pe)
            return mark(dve)

        q1_done = rope_proj(ropeq, dWQ1)
        q2_done = rope_proj(ropet, dWQ2)
        for c in range(4):
            p.emit("vector", lambda e, c=c: e.tensor_tensor(qr12[:, c * S:(c + 1) * S], ropeq[c][:, :],
                                                            cf[:, :], ALU.mult),
                   waits=[q1_done, q2_done, consts_ld], inc=dve, raw=True)
            p.emit("vector", lambda e, c=c: e.tensor_tensor(ropeq[c][:, :], ropet[c][:, :], sf[:, :],
                                                            ALU.mult), inc=dve, raw=True)
            p.emit("vector", lambda e, c=c: e.tensor_tensor(qr12[:, c * S:(c + 1) * S],
                                                            qr12[:, c * S:(c + 1) * S], ropeq[c][:, :],
                                                            ALU.subtract), inc=dve, raw=True)
        part1 = mark(dve)
        q1s_done = rope_proj(ropeq, dWQ1s)
        q2s_done = rope_proj(ropet, dWQ2s)
        for c in range(4):
            p.emit("vector", lambda e, c=c: e.tensor_tensor(ropeq[c][:, :], ropeq[c][:, :], sf[:, :], ALU.mult),
                   waits=[q1s_done, q2s_done, part1], inc=dve, raw=True)
            p.emit("vector", lambda e, c=c: e.tensor_tensor(ropet[c][:, :], ropet[c][:, :], cf[:, :], ALU.mult),
                   inc=dve, raw=True)
            p.emit("vector", lambda e, c=c: e.tensor_tensor(ropeq[c][:, :], ropeq[c][:, :], ropet[c][:, :],
                                                            ALU.add), inc=dve, raw=True)
            p.emit("vector", lambda e, c=c: e.tensor_tensor(qr12[:, c * S:(c + 1) * S],
                                                            qr12[:, c * S:(c + 1) * S], ropeq[c][:, :],
                                                            ALU.add), inc=dve, raw=True)
        qr_done = mark(dve)

        wlv = load_w(wbufP3, dWUV.rearrange("(c p) o -> p c o", p=128), 4 * D, 4)
        p.emit("vector", lambda e: e.memset(v_a[:, :], 1.0), waits=[mark(dve), qr_done], inc=dve, raw=True)
        p.emit("vector", lambda e: e.memset(v_b[:, :], 1.0), inc=dve, raw=True)
        vset = mark(dve)
        for kt in range(NT):
            vten = v_a if kt < 8 else v_b
            ktc = kt % 8
            for ob in range(2):
                bk_t = (pC, pD)[ob]
                bk = bank[id(bk_t)]
                for k in range(4):
                    p.emit("tensor", lambda e, kt=kt, ob=ob, k=k, bk_t=bk_t: e.matmul(
                        bk_t[:, :], c_kv[:, k * S + kt * 128: k * S + (kt + 1) * 128],
                        wbufP3[:, k * D + ob * 512: k * D + (ob + 1) * 512],
                        start=(k == 0), stop=(k == 3)),
                        waits=[wlv, ckv_done, bk["free"]], inc=pe)
                mm = mark(pe)
                for hh in range(8):
                    h = ob * 8 + hh
                    p.emit("vector", lambda e, vten=vten, ktc=ktc, h=h, hh=hh, bk_t=bk_t: e.tensor_copy(
                        vten[:, ktc * 1040 + h * 65: ktc * 1040 + h * 65 + 64],
                        bk_t[:, hh * 64:(hh + 1) * 64]),
                        waits=[mm, vset], inc=dve, raw=True)
                bk["free"] = mark(dve)
        v_done = mark(dve)
        sdma("sync", mtab[:, :], d_m[:, :], s_const, waits=[mark(pe)])
        mtab_ld = mark(s_const)

    if upto >= 5:
        sc_banks2 = (scA, scB)
        LAG = 3
        tail_pend = [None]
        rcp_reader = [None]
        exps_reader = {}
        av_banks = (pE, pF)
        for ps in range(2):
            prev_attn = [mark(pe), mark(act), mark(dve)]
            proj_evac = []
            for (wd, dstT, srcT, src_ready) in ((dWUQ, q96, c_q, cq_done),
                                                (dWUK, k96, c_kv, ckv_done)):
                sdma("gpsimd", wbufQ[:, :].rearrange("p (c o) -> p c o", c=4),
                     wd.rearrange("(c p) o -> p c o", p=128), s_wq,
                     waits=prev_attn + [mark(pe)])
                wl = mark(s_wq)
                for i in range(8):
                    h = 8 * ps + i
                    for tt in range(4):
                        bk_t = (pA, pB)[(i * 4 + tt) % 2]
                        bk = bank[id(bk_t)]
                        for k in range(4):
                            p.emit("tensor", lambda e, h=h, tt=tt, k=k, bk_t=bk_t, srcT=srcT: e.matmul(
                                bk_t[0:64, :], wbufQ[:, k * D + 64 * h: k * D + 64 * h + 64],
                                srcT[:, k * S + tt * 512: k * S + (tt + 1) * 512],
                                start=(k == 0), stop=(k == 3)),
                                waits=[wl, src_ready, bk["free"]], inc=pe)
                        mm = mark(pe)
                        p.emit("vector", lambda e, i=i, tt=tt, bk_t=bk_t, dstT=dstT: e.tensor_copy(
                            dstT[0:64, i * S + tt * 512: i * S + (tt + 1) * 512], bk_t[0:64, :]),
                            waits=[mm, scat_done[0], scat_done[1]], inc=dve, raw=True)
                        bk["free"] = mark(dve)
            proj_done = mark(dve)
            for i in range(8):
                h = 8 * ps + i
                rc4, rb4 = rope_slot(h)
                sdma("sync", q96[64:96, i * S:(i + 1) * S],
                     qr12[rb4:rb4 + 32, rc4 * S:(rc4 + 1) * S], s_pk,
                     waits=[qr_done] + prev_attn + scat_done)
                sdma("sync", k96[64:96, i * S:(i + 1) * S], kr12[0:32, :], s_pk,
                     waits=[kr_done] + prev_attn + scat_done)
            pack_done = mark(s_pk)
            ready = [proj_done, pack_done, v_done, consts_ld, mtab_ld]
            for i in range(8):
                h = 8 * ps + i
                for qb in range(4):
                    nkt = 4 * qb + 4
                    avb = av_banks[(h * 4 + qb) % 2]
                    avbk = bank[id(avb)]
                    pend = []

                    def emit_av(h=h, qb=qb, nkt=nkt, avb=avb, avbk=avbk):
                        kt, sl, use_a, rdy = pend.pop(0)
                        src = exps_a if use_a else exps_b
                        p.emit("tensor", lambda e, h=h, kt=kt, sl=sl, src=src, avb=avb, nkt=nkt: e.matmul(
                            avb[0:65, :],
                            (v_a if kt < 8 else v_b)[:, (kt % 8) * 1040 + h * 65: (kt % 8) * 1040 + h * 65 + 65],
                            src[:, sl * 512:(sl + 1) * 512],
                            start=(kt == 0), stop=(kt == nkt - 1)),
                            waits=[rdy] + ([avbk["free"]] if kt == 0 else []), inc=pe)
                        exps_reader[sl] = mark(pe)

                    for kt2 in range(0, nkt, 2):
                        sb2 = sc_banks2[(kt2 // 2) % 2]
                        sbk = bank[id(sb2)]
                        sl2 = (kt2 // 2) % 2
                        for j in range(2):
                            kt = kt2 + j
                            p.emit("tensor", lambda e, i=i, kt=kt, qb=qb, sb2=sb2, j=j: e.matmul(
                                sb2[:, j * 512:(j + 1) * 512],
                                k96[0:96, i * S + kt * 128: i * S + (kt + 1) * 128],
                                q96[0:96, i * S + qb * 512: i * S + (qb + 1) * 512],
                                start=True, stop=True),
                                waits=ready + ([sbk["free"]] if j == 0 else []), inc=pe)
                        smm = mark(pe)
                        p.emit("scalar", lambda e, sl2=sl2, sb2=sb2: e.activation(
                            exps_a[:, sl2 * 1024:(sl2 + 1) * 1024], sb2[:, :], AF.Exp, scale=SCALE),
                            waits=[smm, exps_reader.get(2 * sl2), exps_reader.get(2 * sl2 + 1)],
                            inc=act, raw=True)
                        exm = mark(act)
                        sbk["free"] = exm
                        p.emit("vector", lambda e, sl2=sl2: e.tensor_scalar(
                            exps_b[:, sl2 * 1024:(sl2 + 1) * 1024], exps_a[:, sl2 * 1024:(sl2 + 1) * 1024],
                            CLO, CHI, ALU.max, ALU.min),
                            waits=[exm], inc=dve)
                        for j in range(2):
                            kt = kt2 + j
                            sl = kt % 4
                            partial = kt >= 4 * qb
                            if partial:
                                mo = qb * 512 - kt * 128 + 384
                                p.emit("vector", lambda e, sl=sl, mo=mo: e.tensor_tensor(
                                    exps_a[:, sl * 512:(sl + 1) * 512], exps_b[:, sl * 512:(sl + 1) * 512],
                                    mtab[:, mo:mo + 512], ALU.mult),
                                    waits=[mtab_ld], inc=dve, raw=True)
                            pend.append((kt, sl, partial, mark(dve)))
                        if kt2 == 2 and tail_pend[0] is not None:
                            tail_pend[0]()
                            tail_pend[0] = None
                        while len(pend) > LAG:
                            emit_av()
                    if tail_pend[0] is not None:
                        tail_pend[0]()
                        tail_pend[0] = None
                    while pend:
                        emit_av()
                    avm = mark(pe)
                    p.emit("vector", lambda e, avb=avb: e.reciprocal(rcp[:, :], avb[64:65, :]),
                           waits=[avm, rcp_reader[0]], inc=dve, raw=True)
                    rc = mark(dve)

                    def emit_tail(h=h, qb=qb, avb=avb, avbk=avbk, rc=rc):
                        ch = h // 2
                        rb = 64 * (h % 2)
                        bkD = bank[id(pG)]
                        p.emit("tensor", lambda e: e.matmul(pG[0:64, :], orowf[:, :], rcp[:, :],
                                                            start=True, stop=True),
                               waits=[rc, bkD["free"], consts_ld], inc=pe)
                        bc = mark(pe)
                        rcp_reader[0] = bc
                        p.emit("vector", lambda e: e.tensor_copy(rbsb[:, :], pG[0:64, :]),
                               waits=[bc], inc=dve, raw=True)
                        bkD["free"] = mark(dve)
                        p.emit("vector", lambda e, ch=ch, rb=rb, qb=qb, avb=avb: e.tensor_tensor(
                            attn_fm[rb:rb + 64, ch * S + qb * 512: ch * S + (qb + 1) * 512],
                            avb[0:64, :], rbsb[:, :], ALU.mult), inc=dve, raw=True)
                        avbk["free"] = mark(dve)
                    tail_pend[0] = emit_tail
                if ps == 1 and i == 3:
                    wo_gate = mark(pe)
        if tail_pend[0] is not None:
            tail_pend[0]()
            tail_pend[0] = None
        attn_done = mark(dve)
        attn_pe_end = mark(pe)
    else:
        wo_gate = None

    if upto >= 6:
        wlo = load_w(wobuf, dWO.rearrange("(c p) o -> p c o", p=128), 8 * D, 8,
                     conservative=False, extra=[wo_gate])
        wo_banks = (pA, pB, pF, pG)
        st_bufs = [f32c, f32d]
        st_reader = [None, None]
        base_stores = []
        for t in range(NT):
            b = t % 2
            sdma("sync", xt[b][:, :], dx[t * 128:(t + 1) * 128, :], s_x[b], waits=[xt_reader[b]])
            xl = mark(s_x[b])
            both = []
            for dh in range(2):
                bk_t = wo_banks[(t * 2 + dh) % 4]
                bk = bank[id(bk_t)]
                for c in range(8):
                    p.emit("tensor", lambda e, t=t, dh=dh, c=c, bk_t=bk_t: e.matmul(
                        bk_t[:, :], attn_fm[:, c * S + t * 128: c * S + (t + 1) * 128],
                        wobuf[:, c * D + dh * 512: c * D + (dh + 1) * 512],
                        start=(c == 0), stop=(c == 7)),
                        waits=[wlo, attn_done, bk["free"]], inc=pe)
                mm = mark(pe)
                p.emit("vector", lambda e, b=b, dh=dh, bk_t=bk_t: e.tensor_tensor(
                    st_bufs[b][:, dh * 512:(dh + 1) * 512], bk_t[:, :],
                    xt[b][:, dh * 512:(dh + 1) * 512], ALU.add),
                    waits=[mm, xl, st_reader[b]], inc=dve, raw=True)
                bk["free"] = mark(dve)
            bs = mark(dve)
            xt_reader[b] = bs
            sdma("sync", d_base[t * 128:(t + 1) * 128, :], st_bufs[b][:, :], s_scd[b], waits=[bs])
            st_reader[b] = mark(s_scd[b])
            base_stores.append(mark(s_scd[b]))
        base_done = [mark(s_scd[0]), mark(s_scd[1])]

    ETE = mybir.EngineType
    eng_et = {"sync": ETE.SP, "tensor": ETE.PE, "scalar": ETE.Activation,
              "vector": ETE.DVE, "gpsimd": ETE.Pool}
    guard_state = {}

    def if_open(eng, fidx, gate):
        guard_state[eng] = {k: v for k, v in p.waited.items() if k[0] == eng}
        cell = {}
        guard_state[(eng, "cell")] = cell

        def th(e, fidx=fidx, gate=gate, et=eng_et[eng], cell=cell):
            e.wait_ge(gate[0].h, gate[1])
            val = nc.values_load(flgi[0:1, fidx:fidx + 1], engines=[et],
                                 min_val=0, max_val=1, skip_runtime_bounds_check=True)
            g = nc.If(val)
            g.__enter__()
            cell["g"] = g
        p.streams[eng].append(th)

    def if_close(eng, deltas=None):
        cell = guard_state[(eng, "cell")]

        def th(e, cell=cell, deltas=deltas):
            cell["g"].__exit__(None, None, None)
            if deltas:
                ge = nc.Else()
                ge.__enter__()
                for (h, d) in deltas:
                    e.memset(junk[0:1, 0:1], 0.0).then_inc(h, d)
                ge.__exit__(None, None, None)
        p.streams[eng].append(th)
        snap = guard_state[eng]
        for k in [k for k in p.waited if k[0] == eng]:
            del p.waited[k]
        p.waited.update(snap)

    if upto >= 7:
        wA_m = wB_m = b1_m = b2_m = None
        cur_par = 0

        def load_w1(ei, waits, waits_b=None):
            nonlocal wA_m, b1_m, b2_m
            sdma("gpsimd", wAa[:, :].rearrange("p (c i) -> p c i", c=8),
                 dw1[ei, :, 0:2048].rearrange("(c p) i -> p c i", p=128), s_wAa, waits=waits)
            sdma("gpsimd", wAb[:, :].rearrange("p (c i) -> p c i", c=8),
                 dw1[ei, :, 2048:4096].rearrange("(c p) i -> p c i", p=128), s_wAb,
                 waits=(waits_b if waits_b is not None else waits))
            wA_m = [mark(s_wAa), mark(s_wAb)]
            par = ei % 2
            sdma("sync", b1sb[:, par * 32:(par + 1) * 32], db1[ei, :, :], s_b1, waits=waits)
            b1_m = mark(s_b1)
            sdma("gpsimd", b2sb[:, par * D:(par + 1) * D], db2[ei:ei + 1, :], s_b2, waits=waits)
            b2_m = mark(s_b2)

        def load_w2(ei, waits, waits_a=None):
            nonlocal wB_m
            sdma("gpsimd", wBa[:, :].rearrange("p (c o) -> p c o", c=16),
                 dw2[ei, 0:2048, :].rearrange("(c p) o -> p c o", p=128), s_wBa,
                 waits=(waits_a if waits_a is not None else waits))
            sdma("gpsimd", wBb[:, :].rearrange("p (c o) -> p c o", c=16),
                 dw2[ei, 2048:4096, :].rearrange("(c p) o -> p c o", p=128), s_wBb, waits=waits)
            wB_m = [mark(s_wBa), mark(s_wBb)]

        def load_expert(ei, waits):
            load_w1(ei, waits)
            load_w2(ei, waits)

        def w1_tile(k, ic):
            ten = wAa if ic < 16 else wAb
            return ten[:, k * 2048 + (ic % 16) * 128: k * 2048 + (ic % 16) * 128 + 128]

        def w2_tile(ic2, hh):
            ten = wBa if ic2 < 16 else wBb
            return ten[:, (ic2 % 16) * D + hh * 512: (ic2 % 16) * D + (hh + 1) * 512]

        load_w1(0, [attn_pe_end] + scat_done,
                waits_b=[mark(pe)] + scat_done)
        load_w2(0, [mark(pe), mark(act), mark(dve)] + base_done)
        xg_reader = attn_pe_end
        hm_banks = (pA, pB)
        ho_banks = (pC, pD, pE, pF)
        hout_buf = [f32c, f32d]
        hout_reader = [st_reader[0], st_reader[1]]
        ffn_stores = []
        for ei in range(E):
            par = ei % 2
            soff = ei * CAP
            boff = 0
            wA_cur, b1_cur, b2_cur = wA_m, b1_m, b2_m
            wB_cur = wB_m
            w1_gate = w1_gate_a = None
            w1a_mark = w2a_mark = None
            for bi, bs_n in enumerate(BLKS):
                nct = bs_n // 128
                fidx = bi * E + ei
                for ge_ in ("sync", "tensor", "scalar", "vector"):
                    if_open(ge_, fidx, flag_gate)
                sem_snap = {id(s): s.n for s in p.all_sems}
                for c in range(nct):
                    sdma("sync", xg_st[:, c * D:(c + 1) * D],
                         d_xg[soff + boff + c * 128: soff + boff + (c + 1) * 128, :], s_xg,
                         waits=scat_done + [xg_reader])
                gl = mark(s_xg)
                bkT = bank[id(pT)]
                for c in range(nct):
                    for dc in range(8):
                        p.emit("tensor", lambda e, c=c, dc=dc: e.transpose(
                            pT[:, 0:128], xg_st[:, c * D + dc * 128: c * D + (dc + 1) * 128], ident[:, :]),
                            waits=[gl, bkT["free"]], inc=pe)
                        tm = mark(pe)
                        p.emit("scalar", lambda e, c=c, dc=dc, bs_n=bs_n: e.copy(
                            xg_fm[:, dc * bs_n + c * 128: dc * bs_n + (c + 1) * 128], pT[:, 0:128]),
                            waits=[tm], inc=act, raw=True)
                        bkT["free"] = mark(act)
                xg_done = mark(act)
                xg_reader = xg_done
                for ic in range(32):
                    bk_t = hm_banks[ic % 2]
                    bk = bank[id(bk_t)]
                    for k in range(8):
                        p.emit("tensor", lambda e, ic=ic, k=k, bk_t=bk_t, bs_n=bs_n: e.matmul(
                            bk_t[:, 0:bs_n], w1_tile(k, ic), xg_fm[:, k * bs_n:(k + 1) * bs_n],
                            start=(k == 0), stop=(k == 7)),
                            waits=[wA_cur[0 if ic < 16 else 1], xg_done, bk["free"]], inc=pe)
                    if ic == 15:
                        w1a_mark = mark(pe)
                    mm = mark(pe)
                    p.emit("scalar", lambda e, ic=ic, bk_t=bk_t, bs_n=bs_n, par=par: e.activation(
                        hmid[:, ic * 256: ic * 256 + bs_n], bk_t[:, 0:bs_n], AF.Gelu,
                        bias=b1sb[:, par * 32 + ic: par * 32 + ic + 1]),
                        waits=[mm, b1_cur], inc=act, raw=True)
                    bk["free"] = mark(act)
                hm_done = mark(act)
                if bi == len(BLKS) - 1:
                    w1_gate = [mark(pe), mark(act)]
                    w1_gate_a = [w1a_mark, mark(act)]
                for st in range(nct):
                    hv_marks = []
                    for hh in range(2):
                        bk_t = ho_banks[(st * 2 + hh) % 4]
                        bk = bank[id(bk_t)]
                        for ic2 in range(32):
                            p.emit("tensor", lambda e, st=st, hh=hh, ic2=ic2, bk_t=bk_t: e.matmul(
                                bk_t[:, :], hmid[:, ic2 * 256 + st * 128: ic2 * 256 + (st + 1) * 128],
                                w2_tile(ic2, hh), start=(ic2 == 0), stop=False),
                                waits=[wB_cur[0 if ic2 < 16 else 1], hm_done, bk["free"]], inc=pe)
                            if ic2 == 15 and st == nct - 1 and hh == 1:
                                w2a_mark = mark(pe)
                        p.emit("tensor", lambda e, hh=hh, par=par, bk_t=bk_t: e.matmul(
                            bk_t[:, :], orow[:, :], b2sb[:, par * D + hh * 512: par * D + (hh + 1) * 512],
                            start=False, stop=True),
                            waits=[b2_cur], inc=pe)
                        mm = mark(pe)
                        hb = st % 2
                        p.emit("vector", lambda e, hb=hb, hh=hh, bk_t=bk_t: e.tensor_copy(
                            hout_buf[hb][:, hh * 512:(hh + 1) * 512], bk_t[:, :]),
                            waits=[mm, hout_reader[hb]], inc=dve, raw=True)
                        bk["free"] = mark(dve)
                    hv = mark(dve)
                    hb = st % 2
                    sdma("sync", d_ffn[soff + boff + st * 128: soff + boff + (st + 1) * 128, :],
                         hout_buf[hb][:, :], s_scd[hb], waits=[hv])
                    hout_reader[hb] = mark(s_scd[hb])
                deltas = [(s.h, s.n - sem_snap[id(s)]) for s in p.all_sems
                          if s.n > sem_snap[id(s)]]
                for ge_ in ("sync", "tensor", "scalar"):
                    if_close(ge_)
                if_close("vector", deltas=deltas)
                boff += bs_n
            if ei + 1 < E:
                load_w1(ei + 1, w1_gate_a, waits_b=w1_gate)
                load_w2(ei + 1, [mark(pe)], waits_a=[w2a_mark])
        ffn_done = [mark(s_scd[0]), mark(s_scd[1])]

    if upto >= 8:
        ep_ld = [f32a, f32b]
        ep_g = [f32c, f32d]
        ld_reader = [xt_reader[0], xt_reader[1]]
        g_reader = [hout_reader[0], hout_reader[1]]
        for t in range(NT):
            b = t % 2
            sdma("sync", ep_ld[b][:, :], d_base[t * 128:(t + 1) * 128, :], s_x[b],
                 waits=[ld_reader[b], base_stores[t]])
            bl = mark(s_x[b])
            p.emit("vector", lambda e, b=b: e.memset(ep_g[b][:, :], 0.0), waits=[g_reader[b]], inc=dve, raw=True)
            ms = mark(dve)
            p.emit("gpsimd", lambda e, t=t, b=b: e.indirect_dma_start(
                ep_g[b][:, :], None, d_ffn[0:NSLOT, :], IndirectOffsetOnAxis(slot[:, t:t + 1], 0),
                bounds_check=NSLOT - 1, oob_is_err=False),
                waits=[ms] + ffn_done, inc=s_gath[b], k=16)
            gm = mark(s_gath[b])
            p.emit("vector", lambda e, t=t, b=b: e.scalar_tensor_tensor(
                ep_ld[b][:, :], ep_g[b][:, :], gate[:, t:t + 1], ep_ld[b][:, :], ALU.mult, ALU.add),
                waits=[bl, gm], inc=dve, raw=True)
            om = mark(dve)
            g_reader[b] = om
            sdma("sync", dout[t * 128:(t + 1) * 128, :], ep_ld[b][:, :], s_sab[b], waits=[om])
            ld_reader[b] = mark(s_sab[b])

    if dbg:
        reg = {
            "xn_fm": (xn_fm, 128, 8 * S, BF16),
            "gate": (gate, 128, NT, F32),
            "idxf": (idxf, 128, NT, F32),
            "logit": (logit, 128, E, F32),
            "slot": (slot, 128, NT, I32),
            "c_q": (c_q, 128, 4 * S, BF16),
            "c_kv": (c_kv, 128, 4 * S, BF16),
            "kr12": (kr12, 32, S, BF16),
            "q96": (q96, 128, 8 * S, BF16),
            "qr12": (qr12, 128, 4 * S, BF16),
            "k96": (k96, 128, 8 * S, BF16),
            "v_a": (v_a, 128, 8 * 1040, BF16),
            "v_b": (v_b, 128, 8 * 1040, BF16),
            "attn_fm": (attn_fm, 128, 8 * S, BF16),
            "base_d": (d_base, S, D, F32),
            "ffn_d": (d_ffn, NSLOT, D, F32),
            "xg_d": (d_xg, NSLOT, D, BF16),
            "xn_tok": (d_xn, S, D, BF16),
        }
        allsems = [pe, dve, act, s_const, s_wr, s_w1, s_w2, s_w3, s_wo, s_wAa, s_wAb,
                   s_wBa, s_wBb, s_b1, s_b2, s_xg] + s_scat + s_x + s_xnb + s_sxnb +             s_scd + s_sab + s_gath
        allw = [mark(s) for s in allsems]
        for name in dbg:
            ten, rows, cols, dt = reg[name]
            dd = nc.dram_tensor("dbg_" + name, [rows, cols], dt, kind="ExternalOutput").ap()
            sdma("sync", dd[:, :], ten[0:rows, 0:cols], s_sab[0], waits=allw)

    finsems = [s_sab[0], s_sab[1], s_scd[0], s_scd[1], s_gath[0], s_gath[1]] + s_scat
    p.emit("sync", lambda e: None, waits=[mark(s) for s in finsems])

    with nc.Block() as block:
        @block.sync
        def _(eng):
            for f in p.streams["sync"]:
                f(eng)

        @block.gpsimd
        def _(eng):
            for f in p.streams["gpsimd"]:
                f(eng)

        @block.tensor
        def _(eng):
            for f in p.streams["tensor"]:
                f(eng)

        @block.vector
        def _(eng):
            for f in p.streams["vector"]:
                f(eng)

        @block.scalar
        def _(eng):
            for f in p.streams["scalar"]:
                f(eng)

    return nc


def _host_weights(inp):
    o = {}
    o["wdq_t"] = np.ascontiguousarray(inp["L0_attn_W_DQ"].T)
    o["wdkv_t"] = np.ascontiguousarray(inp["L0_attn_W_DKV"].T)
    o["wuq_t"] = np.ascontiguousarray(inp["L0_attn_W_UQ"].T)
    o["wuk_t"] = np.ascontiguousarray(inp["L0_attn_W_UK"].T)
    o["wuv_t"] = np.ascontiguousarray(inp["L0_attn_W_UV"].T)
    o["wo_t"] = np.ascontiguousarray(inp["L0_attn_W_O"].T)
    wqr_t = inp["L0_attn_W_QR"].T
    q1 = np.zeros((DLAT, 512), np.float32)
    q2 = np.zeros((DLAT, 512), np.float32)
    q1s = np.zeros((DLAT, 512), np.float32)
    q2s = np.zeros((DLAT, 512), np.float32)
    for h in range(H):
        c4, rb4 = rope_slot(h)
        co = c4 * 128 + rb4
        q1[:, co:co + 16] = wqr_t[:, 32 * h + 0:32 * h + 32:2]
        q2[:, co:co + 16] = wqr_t[:, 32 * h + 1:32 * h + 32:2]
        q1s[:, co + 16:co + 32] = wqr_t[:, 32 * h + 0:32 * h + 32:2]
        q2s[:, co + 16:co + 32] = wqr_t[:, 32 * h + 1:32 * h + 32:2]
    o["wqr1"], o["wqr2"], o["wqr1s"], o["wqr2s"] = q1, q2, q1s, q2s
    wkr_t = inp["L0_attn_W_KR"].T
    kp = np.zeros((D, 64), np.float32)
    kps = np.zeros((D, 64), np.float32)
    kp[:, 0:16] = wkr_t[:, 0::2]
    kp[:, 32:48] = wkr_t[:, 1::2]
    kps[:, 16:32] = wkr_t[:, 1::2]
    kps[:, 48:64] = wkr_t[:, 0::2]
    o["wkrp"], o["wkrps"] = kp, kps
    o["wr"] = np.ascontiguousarray(inp["L0_ffn_wr"])
    o["br"] = np.ascontiguousarray(inp["L0_ffn_br"].reshape(1, E))
    o["w1"] = np.ascontiguousarray(inp["L0_ffn_w1"])
    o["b1t"] = np.ascontiguousarray(inp["L0_ffn_b1"].reshape(E, 32, 128).transpose(0, 2, 1))
    o["w2"] = np.ascontiguousarray(inp["L0_ffn_w2"])
    o["b2"] = np.ascontiguousarray(inp["L0_ffn_b2"])
    return o


def kernel(**inputs):
    x = inputs["L0_attn_x"]
    w = _host_weights(inputs)
    nc = build_core()
    in_maps = []
    for b in range(8):
        m = {"x": np.ascontiguousarray(x[b])}
        m.update(w)
        in_maps.append(m)
    res = run_bass_kernel_spmd(nc, in_maps, core_ids=list(range(8)))
    return np.stack([r["out"] for r in res.results], axis=0)


if __name__ == "__main__":
    nc = build_core()
    print("built ok; instructions:", nc.n_instructions())
    print("sbuf remaining:", nc.sbuf_bytes_remaining)
